# version 21
# baseline (speedup 1.0000x reference)
import math
from contextlib import ExitStack

import numpy as np
import ml_dtypes

import concourse.bass as bass
import concourse.mybir as mybir
from concourse.bass_utils import run_bass_kernel_spmd

F32 = mybir.dt.float32
BF16 = mybir.dt.bfloat16
AF = mybir.ActivationFunctionType
ALU = mybir.AluOpType
AX = mybir.AxisListType

BIG = 3.0e38
ENGINES = ("tensor", "vector", "scalar", "sync", "gpsimd")


def real_cfg():
    return dict(B=2, T=2048, C=1024, H=16, HS=64, DH=4096,
                sweeps=(1024, 1024, 1024, 1024))


class Sem:
    __slots__ = ("h", "v", "name")

    def __init__(self, h, name):
        self.h = h
        self.v = 0
        self.name = name


class Prog:
    def __init__(self):
        self.q = {e: [] for e in ENGINES}
        self.waited = {e: {} for e in ENGINES}
        self.self_sem = {}

    def emit(self, eng, fn, wait=(), inc=None, inc_n=1):
        wait = list(wait)
        if inc is not None and inc_n >= 16 and inc.v > 0:
            wait.append((inc, inc.v))
        ss = self.self_sem.get(eng)
        if ss is not None and not (inc is not None and inc_n >= 16):
            assert inc is None or inc is ss, f"{eng} must inc its engine sem"
            if ss.v > 0:
                wait.append((ss, ss.v))
            inc, inc_n = ss, 1
        w = []
        for item in wait:
            if item is None:
                continue
            s, v = item
            if v <= 0:
                continue
            seen = self.waited[eng].get(id(s), 0)
            if v > seen:
                self.waited[eng][id(s)] = v
                w.append((s, v))
        tgt = None
        if inc is not None:
            inc.v += inc_n
            tgt = inc.v
        self.q[eng].append((tuple(w), fn, inc, inc_n))
        return (inc, tgt) if inc is not None else None


def ceil_div(a, b):
    return (a + b - 1) // b


def build(cfg):
    B, T, C, H, HS, DH = (cfg[k] for k in ("B", "T", "C", "H", "HS", "DH"))
    sweeps = cfg["sweeps"]
    assert HS == 64 and H % 2 == 0 and H * HS == C
    assert C % 128 == 0 and T % 128 == 0 and DH % 128 == 0
    assert sum(sweeps) == B * T and all(s % 128 == 0 for s in sweeps)
    NCC = C // 128
    NQT = T // 128
    NHP = H // 2
    TOK = B * T
    ND = DH // 128
    SWMAX = max(sweeps)
    NBAV = ceil_div(H, 7)
    AVB = 8 - NBAV
    nstat = ceil_div(C, 512)

    nc = bass.Bass("TRN2", target_bir_lowering=False)

    dt = nc.dram_tensor
    x_in = dt("x", [TOK, C], F32, kind="ExternalInput").ap()
    wqkv = dt("wqkv", [H, 3 * HS, C], F32, kind="ExternalInput").ap()
    wproj = dt("wproj", [C, C], F32, kind="ExternalInput").ap()
    bproj_bc = dt("bproj_bc", [128, C], F32, kind="ExternalInput").ap()
    w1 = dt("w1", [DH, C], F32, kind="ExternalInput").ap()
    w2 = dt("w2", [DH, C], F32, kind="ExternalInput").ap()
    h1 = dt("h1", [DH, DH], F32, kind="ExternalInput").ap()
    h2 = dt("h2", [DH, DH], F32, kind="ExternalInput").ap()
    w3 = dt("w3", [C, DH], F32, kind="ExternalInput").ap()
    w1s = dt("w1s", [128, ND], F32, kind="ExternalInput").ap()
    w2s = dt("w2s", [128, ND], F32, kind="ExternalInput").ap()
    h1s = dt("h1s", [128, ND], F32, kind="ExternalInput").ap()
    h2s = dt("h2s", [128, ND], F32, kind="ExternalInput").ap()
    w3s = dt("w3s", [128, NCC], F32, kind="ExternalInput").ap()
    ident_in = dt("ident", [128, 128], F32, kind="ExternalInput").ap()
    binT_in = dt("binT", [128, 128], F32, kind="ExternalInput").ap()
    identf_in = dt("identf", [128, 128], F32, kind="ExternalInput").ap()
    out_hbm = dt("out", [TOK, C], F32, kind="ExternalOutput").ap()

    x1_hbm = dt("x1_scratch", [TOK, C], F32, kind="Internal").ap()
    qkvB_hbm = dt("qkvB", [NHP, NCC * 384, 128], BF16, kind="Internal").ap()
    projB_hbm = dt("projB", [NCC, NCC * 128, 128], BF16, kind="Internal").ap()
    w1B_hbm = dt("w1B", [ND, NCC * 128, 128], BF16, kind="Internal").ap()
    w2B_hbm = dt("w2B", [ND, NCC * 128, 128], BF16, kind="Internal").ap()
    h1B_hbm = dt("h1B", [ND, ND * 128, 128], BF16, kind="Internal").ap()
    h2B_hbm = dt("h2B", [ND, ND * 128, 128], BF16, kind="Internal").ap()
    w3B_hbm = dt("w3B", [NCC, ND * 128, 128], BF16, kind="Internal").ap()

    st = ExitStack()
    class Alloc:
        def __init__(self):
            self.off = 0

        def take(self, nbytes, align=4):
            self.off = (self.off + align - 1) // align * align
            o = self.off
            self.off += nbytes
            return o

    al = Alloc()
    o_ident = al.take(128 * 2)
    o_binT = al.take(128 * 2)
    o_identf = al.take(128 * 4)
    o_bbc = al.take(C * 4)
    o_w1s = al.take(ND * 4)
    o_w2s = al.take(ND * 4)
    o_h1s = al.take(ND * 4)
    o_h2s = al.take(ND * 4)
    o_w3s = al.take(NCC * 4)
    o_stats = al.take(4 * 12 * 4)
    o_mv = al.take(4 * 2 * 4)
    o_std = al.take(4 * 4)
    o_istd = al.take(4 * 4)
    o_lntok = al.take(2 * C * 2)
    o_nmp = al.take(2 * 4 * 4)
    o_rsp = al.take(2 * 4 * 4)
    o_rs = al.take(H * 4)
    o_rl = al.take(H * 4)
    p0 = al.off
    al.off = p0
    o_xlnA = al.take(NCC * T * 2)
    o_Q = al.take(NHP * T * 2)
    o_K = al.take(NHP * T * 2)
    o_V = al.take((T // 128) * H * 64 * 2)
    o_P = al.take(2 * T * 2)
    o_PT = al.take(2 * 512 * 2)
    o_qpad = al.take(2 * 128 * 2)
    o_atok = al.take(C * 2)
    o_atT = al.take(C * 2)
    o_wqkvA = al.take(2 * NCC * 384 * 2, align=32)
    o_wprojA = al.take(NCC * C * 2, align=32)
    o_xA = al.take(2 * C * 4)
    o_x1A = al.take(2 * C * 4)
    endA = al.off
    al.off = p0
    o_xln2 = al.take(NCC * SWMAX * 2)
    o_h = al.take(ND * SWMAX * 2)
    o_g = al.take(ND * SWMAX * 2)
    o_wfA = al.take(2 * ND * 128 * 2, align=32)
    o_wfB = al.take(2 * NCC * 128 * 2, align=32)
    o_silu = al.take(SWMAX * 2)
    o_xF = al.take(2 * C * 4)
    o_x2F = al.take(2 * C * 4)
    endF = al.off
    total = max(endA, endF)
    arena_elems = ceil_div(total, 2) + 64

    arena = st.enter_context(nc.sbuf_tensor("arena", [128, arena_elems], BF16))
    ps_t = st.enter_context(nc.psum_tensor("ps", [128, 4096], F32))
    A = arena.ap()
    PS = ps_t.ap()

    def sb(off, n, dtype=BF16, parts=(0, 128)):
        if dtype == BF16:
            v = A[parts[0]:parts[1], off // 2: off // 2 + n]
        else:
            v = A[parts[0]:parts[1], off // 2: off // 2 + 2 * n].bitcast(F32)
        return v

    def bank(i, w=512):
        return PS[:, i * 512: i * 512 + w]

    def bankb(i, w=1024):
        return PS[:, i * 512: (i + 1) * 512].bitcast(BF16)[:, :w]

    ident = sb(o_ident, 128)
    mtri = sb(o_binT, 128)
    identf = sb(o_identf, 128, F32)
    bbc = sb(o_bbc, C, F32)

    prog = Prog()
    sems = []

    def mksem(name):
        s = Sem(st.enter_context(nc.semaphore(name)), name)
        sems.append(s)
        return s

    d_x = mksem("d_x")
    d_x2 = mksem("d_x2")
    d_st = mksem("d_st")
    d_w = mksem("d_w")
    d_t = [mksem(f"d_t{i}") for i in range(6)]
    t_mm = mksem("t_mm")
    t_s = mksem("t_s")
    t_pt = mksem("t_pt")
    t_av = mksem("t_av")
    t_tr = mksem("t_tr")
    v_e = mksem("v")
    v_red = v_fin = v_res = v_e
    s_e = mksem("s")
    s_exp = s_wt = s_e
    prog.self_sem = {"vector": v_e, "scalar": s_e}

    gbank = {i: None for i in range(8)}
    tw_lane = [0, 0]

    def em(eng, fn, wait=(), inc=None, inc_n=1):
        return prog.emit(eng, fn, wait, inc, inc_n)

    em("gpsimd", lambda g: g.dma_start(out=ident, in_=ident_in), inc=d_w, inc_n=16)
    em("gpsimd", lambda g: g.dma_start(out=mtri, in_=binT_in), inc=d_w, inc_n=16)
    em("scalar", lambda s: s.dma_start(out=identf, in_=identf_in), inc=d_x, inc_n=16)
    em("scalar", lambda s: s.dma_start(out=bbc, in_=bproj_bc), inc=d_x, inc_n=16)
    for name, off, src, n in (("w1s", o_w1s, w1s, ND), ("w2s", o_w2s, w2s, ND),
                              ("h1s", o_h1s, h1s, ND), ("h2s", o_h2s, h2s, ND),
                              ("w3s", o_w3s, w3s, NCC)):
        dstv = sb(off, n, F32)
        em("scalar", lambda s, d=dstv, sv=src: s.dma_start(out=d, in_=sv),
           inc=d_x, inc_n=16)
    c_after_const = (d_w, d_w.v)
    c_bbc = (d_x, d_x.v)

    wqkvA = [sb(o_wqkvA + k * NCC * 384 * 2, NCC * 384) for k in range(2)]
    wprojA = sb(o_wprojA, NCC * C)

    qkv_cast = [None] * NHP
    for hp in range(NHP):
        for half in range(2):
            s_ap = wqkv[2 * hp + half].rearrange(
                "(part r) (cc c) -> cc part r c", r=HS, c=128)
            d_ap = qkvB_hbm[hp].rearrange(
                "(cc part half r) c -> cc part half r c", part=3, half=2, r=HS)[
                :, :, half]
            em("gpsimd", lambda g, d=d_ap, s_=s_ap: g.dma_start(out=d, in_=s_),
               inc=d_w, inc_n=16)
        qkv_cast[hp] = (d_w, d_w.v)
    for oc in range(NCC):
        s_ap = wproj[oc * 128:(oc + 1) * 128, :].rearrange(
            "r (cc c) -> cc r c", c=128)
        d_ap = projB_hbm[:, oc * 128:(oc + 1) * 128, :]
        em("gpsimd", lambda g, d=d_ap, s_=s_ap: g.dma_start(out=d, in_=s_),
           inc=d_w, inc_n=16)
    proj_cast = (d_w, d_w.v)
    em("sync", lambda e: e.dma_start(
        out=wprojA, in_=projB_hbm.rearrange("a b c -> (a b) c"), transpose=True),
       wait=(proj_cast,), inc=d_t[0], inc_n=16)
    wproj_ready = (d_t[0], d_t[0].v)

    wready = {}
    for kind, src, dst, nin, nout in (("w1", w1, w1B_hbm, NCC, ND),
                                      ("w2", w2, w2B_hbm, NCC, ND),
                                      ("h1", h1, h1B_hbm, ND, ND),
                                      ("h2", h2, h2B_hbm, ND, ND),
                                      ("w3", w3, w3B_hbm, ND, NCC)):
        for cc in range(nin):
            s_ap = src[:, cc * 128:(cc + 1) * 128].rearrange(
                "(oc r) c -> oc r c", r=128)
            d_ap = dst[:, cc * 128:(cc + 1) * 128, :]
            em("gpsimd", lambda g, d=d_ap, s_=s_ap: g.dma_start(out=d, in_=s_),
               inc=d_w, inc_n=16)
        wready[kind] = (d_w, d_w.v)

    stats4 = [sb(o_stats + k * 12 * 4, 12, F32) for k in range(4)]
    mvs4 = [sb(o_mv + k * 2 * 4, 2, F32) for k in range(4)]
    vars4 = sb(o_mv, 8, F32).rearrange("p (a b) -> p a b", b=2)[:, :, 1:2]
    stdv = sb(o_std, 4, F32)
    istdv = sb(o_istd, 4, F32)

    def emit_ln(src_hbm, row0, ntiles, bufs, xbuf_guard, dst_off, dst_T,
                tbanks=(4, 5)):
        nb = len(bufs)
        lanes_x = (d_x, d_x2)

        def load(i):
            buf = sb(bufs[i % nb], C, F32)
            ln_ = lanes_x[i % 2]
            em("scalar", lambda s, b=buf: s.dma_start(
                out=b, in_=src_hbm[row0 + i * 128: row0 + (i + 1) * 128, :]),
               wait=(xbuf_guard[i % nb],), inc=ln_, inc_n=16)
            return (ln_, ln_.v)

        tile_done = []
        last_ttr = [None, None]
        for g in range(ceil_div(ntiles, 4)):
            tiles = list(range(g * 4, min(g * 4 + 4, ntiles)))
            gn = len(tiles)
            sevs = {}
            for i in tiles:
                ld = load(i)
                buf = sb(bufs[i % nb], C, F32)
                for k in range(nstat):
                    w = min(512, C - k * 512)
                    em("vector", lambda v, st_=stats4[i % 4][:, k * 6:(k + 1) * 6],
                       b=buf[:, k * 512: k * 512 + w]: v.bn_stats(st_, b),
                       wait=(ld,), inc=v_e)
                sevs[i] = (v_e, v_e.v)
            for i in tiles:
                em("vector", lambda v, m=mvs4[i % 4],
                   st_=stats4[i % 4][:, :6 * nstat]: v.bn_aggr(m, st_),
                   wait=(sevs[i],), inc=v_e)
            aggr_ev = (v_e, v_e.v)
            em("vector", lambda v: v.tensor_scalar(
                out=vars4[:, 0:gn], in0=vars4[:, 0:gn], scalar1=1e-5,
                scalar2=None, op0=ALU.add), wait=(aggr_ev,), inc=v_e)
            ve = (v_e, v_e.v)
            em("scalar", lambda sc, o=stdv[:, 0:gn], i_=vars4[:, 0:gn]:
               sc.activation(out=o, in_=i_, func=AF.Sqrt), wait=(ve,), inc=s_e)
            se = (s_e, s_e.v)
            em("vector", lambda v, o=istdv[:, 0:gn], i_=stdv[:, 0:gn]:
               v.reciprocal(o, i_), wait=(se,), inc=v_e)
            for i in tiles:
                buf = sb(bufs[i % nb], C, F32)
                xlntok = sb(o_lntok + (i % 2) * C * 2, C)
                em("vector", lambda v, o=xlntok, b=buf, m=mvs4[i % 4],
                   i_=istdv[:, (i % 4):(i % 4) + 1]:
                   v.tensor_scalar(out=o, in0=b, scalar1=m[:, 0:1], scalar2=i_,
                                   op0=ALU.subtract, op1=ALU.mult),
                   wait=(gbank[tbanks[i % 2]], last_ttr[i % 2]), inc=v_e)
                xbuf_guard[i % nb] = (v_e, v_e.v)
                applied = (v_e, v_e.v)
                bk = tbanks[i % 2]
                for cc in range(NCC):
                    inb = xlntok[:, cc * 128:(cc + 1) * 128]
                    outb = bankb(bk)[:, cc * 128:(cc + 1) * 128]
                    em("tensor", lambda t, o=outb, i_=inb: t.transpose(o, i_, ident),
                       wait=(applied if cc == 0 else None,
                             c_after_const if cc == 0 else None,
                             gbank[bk] if cc == 0 else None),
                       inc=t_tr if cc == NCC - 1 else None)
                ttr = (t_tr, t_tr.v)
                last_ttr[i % 2] = ttr
                dstv = A[:, dst_off // 2: dst_off // 2 + NCC * dst_T].rearrange(
                    "p (cc x) -> p cc x", x=dst_T)[:, :, i * 128:(i + 1) * 128]
                srcv = bankb(bk)[:, :NCC * 128].rearrange("p (cc x) -> p cc x", x=128)
                em("scalar", lambda sc, d=dstv, s=srcv: sc.copy(d, s),
                   wait=(ttr,), inc=s_e)
                gbank[bk] = (s_e, s_e.v)
                tile_done.append((s_e, s_e.v))
        return tile_done

    xlnA_ready = None

    def attention(b):
        nonlocal xlnA_ready
        if b == 0:
            xbuf_guard = [None, None, None, None]
        else:
            xbuf_guard = [attn_res_guard[0], attn_res_guard[1],
                          gbank[("x1A", 0)], gbank[("x1A", 1)]]
        bufs = [o_xA, o_xA + C * 4, o_x1A, o_x1A + C * 4]
        ln_tiles = emit_ln(x_in, b * T, NQT, bufs, xbuf_guard, o_xlnA, T)
        ln_done = ln_tiles[-1]
        xlnA_ready = ln_done

        vavprev = (t_av, t_av.v)
        Voff = o_V // 2
        NT5 = T // 512 if T >= 512 else 1
        W5 = min(512, T)
        for hp in range(NHP):
            slot = hp % 2
            unit = wqkvA[slot]
            ln_q = d_t[tw_lane[0] % len(d_t)]
            tw_lane[0] += 1
            em("sync", lambda e, u=unit, hp_=hp: e.dma_start(
                out=u, in_=qkvB_hbm[hp_], transpose=True),
               wait=(qkv_cast[hp], wq_guard[slot]), inc=ln_q, inc_n=16)
            unit_rdy = (ln_q, ln_q.v)
            uv = unit.rearrange("p (cc x) -> p cc x", x=384)
            for part, dsto, evac_eng in ((0, o_Q, "scalar"), (1, o_K, "vector")):
                for tt in range(NT5):
                    bk = part * 2 + (tt % 2) if False else (0 if part == 0 else 1)
                    bk = (tt % 2) + (0 if part == 0 else 2)
                    for cc in range(NCC):
                        lhs = uv[:, cc, part * 128:(part + 1) * 128]
                        rhs = A[:, (o_xlnA // 2) + cc * T + tt * W5:
                                (o_xlnA // 2) + cc * T + tt * W5 + W5]
                        em("tensor", lambda t, o=bank(bk, W5), l=lhs, r=rhs,
                           st_=(cc == 0), sp=(cc == NCC - 1): t.matmul(
                               o, l, r, start=st_, stop=sp),
                           wait=(unit_rdy if cc == 0 else None,
                                 xlnA_ready if cc == 0 else None,
                                 gbank[bk] if cc == 0 else None),
                           inc=t_mm if cc == NCC - 1 else None)
                    tm = (t_mm, t_mm.v)
                    dst = A[:, (dsto // 2) + hp * T + tt * W5:
                            (dsto // 2) + hp * T + tt * W5 + W5]
                    if part == 0:
                        em("scalar", lambda sc, d=dst, s=bank(bk, W5):
                           sc.mul(d, s, 0.125), wait=(tm,), inc=s_e)
                        gbank[bk] = (s_e, s_e.v)
                    else:
                        em("vector", lambda v, d=dst, s=bank(bk, W5):
                           v.tensor_copy(d, s), wait=(tm,), inc=v_e)
                        gbank[bk] = (v_e, v_e.v)
            for sc_ in range(T // 128):
                bk = 4 + (sc_ % 2)
                for cc in range(NCC):
                    lhs = A[:, (o_xlnA // 2) + cc * T + sc_ * 128:
                            (o_xlnA // 2) + cc * T + (sc_) * 128 + 128]
                    rhs = uv[:, cc, 256:384]
                    em("tensor", lambda t, o=bank(bk, 128), l=lhs, r=rhs,
                       st_=(cc == 0), sp=(cc == NCC - 1): t.matmul(
                           o, l, r, start=st_, stop=sp),
                       wait=(unit_rdy if cc == 0 else None,
                             xlnA_ready if cc == 0 else None,
                             gbank[bk] if cc == 0 else None),
                       inc=t_mm if cc == NCC - 1 else None)
                tm = (t_mm, t_mm.v)
                dst = A[:, Voff + (sc_ * H + 2 * hp) * 64:
                        Voff + (sc_ * H + 2 * hp) * 64 + 128]
                em("vector", lambda v, d=dst, s=bank(bk, 128): v.tensor_copy(d, s),
                   wait=(tm, vavprev), inc=v_e)
                gbank[bk] = (v_e, v_e.v)
            wq_guard[slot] = (t_mm, t_mm.v)
        qkv_done_t = (t_mm, t_mm.v)
        qkv_done_v = (v_e, v_e.v)
        qkv_done_s = (s_e, s_e.v)

        em("scalar", lambda s: s.dma_start(
            out=sb(o_xA, C, F32), in_=x_in[b * T: b * T + 128, :]),
           wait=(xbuf_guard[0],), inc=d_x, inc_n=16)
        xload = {0: (d_x, d_x.v)}
        hpar_guard = [None, None]
        expg = [None, None]
        rsg = [None, None]
        if b == 0:
            em("vector", lambda v: v.memset(
                A[:, o_qpad // 2: o_qpad // 2 + 256], 0.0), inc=v_e)
            qpad_init = (v_e, v_e.v)
            qpad_guard[0] = qpad_guard[1] = qpad_init
        qpad_evs = [None, None]

        def emit_qpad(qt, h_):
            par = h_ % 2
            lo = par * 64
            hpv = (h_ // 2) * T
            qp = (o_qpad // 2) + par * 128
            em("scalar", lambda sc,
               d=A[lo:lo + 64, qp: qp + 128],
               s=A[lo:lo + 64, (o_Q // 2) + hpv + qt * 128:
                   (o_Q // 2) + hpv + qt * 128 + 128]: sc.copy(d, s),
               wait=(qpad_guard[par],), inc=s_e)
            qpad_evs[par] = (s_e, s_e.v)

        def emit_S_mm(qt, h_):
            K_ = (qt + 1) * 128
            npiece = ceil_div(K_, 1024)
            qtw = qt * 128
            pd = qtw // 1024
            hp, lo = h_ // 2, (h_ % 2) * 64
            hpv = hp * T
            par = h_ % 2
            nm = sb(o_nmp + par * 16, 4, F32)
            qp = (o_qpad // 2) + par * 128
            qpad_ev = qpad_evs[par]
            ts_p = []
            for p in range(npiece):
                w_p = min(1024, K_ - p * 1024)
                for jj in range(ceil_div(w_p, 512)):
                    w_j = min(512, w_p - jj * 512)
                    bk = 2 * p + jj
                    off = p * 1024 + jj * 512
                    lhs = A[:, qp: qp + 128]
                    rhs = A[:, (o_K // 2) + hpv + off:
                            (o_K // 2) + hpv + off + w_j]
                    em("tensor", lambda t, o=bank(bk, w_j), l=lhs, r=rhs:
                       t.matmul(o, l, r, start=True, stop=True),
                       wait=(gbank[bk], qpad_ev,
                             qkv_done_v if qt == 0 and h_ == 0 else None,
                             qkv_done_s if qt == 0 and h_ == 0 else None),
                       inc=t_s if jj == ceil_div(w_p, 512) - 1 else None)
                ts_p.append((t_s, t_s.v))
                if p < pd:
                    em("vector", lambda v, o=nm[:, p:p + 1],
                       i_=PS[:, p * 1024: p * 1024 + 1024]:
                       v.tensor_reduce(out=o, in_=i_, axis=AX.X, op=ALU.max,
                                       negate=True),
                       wait=(ts_p[p], expg[h_ % 2]))
            qpad_guard[par] = (t_s, t_s.v)
            em("vector", lambda v, o=PS[:, qtw: qtw + 128], m=mtri:
               v.tensor_tensor(out=o, in0=o, in1=m, op=ALU.add),
               wait=(ts_p[pd], expg[h_ % 2]))
            w_pd = K_ - pd * 1024
            em("vector", lambda v, o=nm[:, pd:pd + 1],
               i_=PS[:, pd * 1024: pd * 1024 + w_pd]:
               v.tensor_reduce(out=o, in_=i_, axis=AX.X, op=ALU.max,
                               negate=True))
            if npiece > 1:
                em("vector", lambda v, o=nm[:, 3:4], i_=nm[:, 0:npiece]:
                   v.tensor_reduce(out=o, in_=i_, axis=AX.X, op=ALU.min,
                                   negate=False))
            nmx = nm[:, 3:4] if npiece > 1 else nm[:, 0:1]
            return nmx, (v_e, v_e.v)

        def emit_exps(qt, h_, nmx, vred):
            K_ = (qt + 1) * 128
            nsc = qt + 1
            par = h_ % 2
            Poff = (o_P // 2) + par * T
            rsp = sb(o_rsp + par * 16, 4, F32)
            nj = ceil_div(K_, 1024)
            chunk_ev = {}
            for j in range(nj):
                w_j = min(1024, K_ - j * 1024)
                em("scalar", lambda sc, o=A[:, Poff + j * 1024: Poff + j * 1024 + w_j],
                   i_=PS[:, j * 1024: j * 1024 + w_j], bi=nmx, a=rsp[:, j:j + 1]:
                   sc.activation(out=o, in_=i_, func=AF.Exp, bias=bi, scale=1.0,
                                 accum_out=a),
                   wait=(vred, hpar_guard[par] if j == 0 else None,
                         rsg[par] if j == 0 else None),
                   inc=s_exp)
                for sc_ in range(j * 8, min(j * 8 + 8, nsc)):
                    chunk_ev[sc_] = (s_exp, s_exp.v)
                gbank[2 * j] = (s_exp, s_exp.v)
                gbank[2 * j + 1] = (s_exp, s_exp.v)
            expg[par] = (s_exp, s_exp.v)
            rs = sb(o_rs, H, F32)
            if nj > 1:
                em("vector", lambda v, o=rs[:, h_:h_ + 1], i_=rsp[:, 0:nj]:
                   v.tensor_reduce(out=o, in_=i_, axis=AX.X, op=ALU.add),
                   wait=(expg[par],), inc=v_e)
            else:
                em("vector", lambda v, o=rs[:, h_:h_ + 1], i_=rsp[:, 0:1]:
                   v.tensor_copy(o, i_), wait=(expg[par],), inc=v_e)
            rsg[par] = (v_e, v_e.v)
            return chunk_ev

        def emit_PTAV(qt, h_, chunk_ev):
            nsc = qt + 1
            ngrp = ceil_div(nsc, 4)
            Poff = (o_P // 2) + (h_ % 2) * T
            avb = 6 + h_ // 8
            avo = 6 * 512 + h_ * 64
            cevs = {}

            def emit_T_C(g_):
                scs = list(range(g_ * 4, min(g_ * 4 + 4, nsc)))
                slot = g_ % 2
                bk = 4 + slot
                for idx, sc_ in enumerate(scs):
                    inb = A[:, Poff + sc_ * 128: Poff + sc_ * 128 + 128]
                    outb = bankb(bk)[:, idx * 128: idx * 128 + 128]
                    em("tensor", lambda t, o=outb, i_=inb: t.transpose(o, i_, ident),
                       wait=(chunk_ev[sc_] if (idx == 0 or sc_ == qt) else None,
                             gbank[("pt", slot)] if idx == 0 else None),
                       inc=t_pt if idx == len(scs) - 1 else None)
                tpt = (t_pt, t_pt.v)
                csem = s_e if g_ % 2 == 0 else v_e
                srcv = bankb(bk)[:, 0: len(scs) * 128]
                dstv = A[:, (o_PT // 2) + slot * 512:
                         (o_PT // 2) + slot * 512 + len(scs) * 128]
                if g_ % 2 == 0:
                    em("scalar", lambda sc, d=dstv, s=srcv: sc.copy(d, s),
                       wait=(tpt, gbank[("PT", slot)]), inc=csem)
                else:
                    em("vector", lambda v, d=dstv, s=srcv: v.tensor_copy(d, s),
                       wait=(tpt, gbank[("PT", slot)]), inc=csem)
                cevs[g_] = (csem, csem.v)
                gbank[("pt", slot)] = cevs[g_]

            def emit_AV(g_):
                scs = list(range(g_ * 4, min(g_ * 4 + 4, nsc)))
                slot = g_ % 2
                for idx, sc_ in enumerate(scs):
                    lhs = A[:, (o_PT // 2) + slot * 512 + idx * 128:
                            (o_PT // 2) + slot * 512 + idx * 128 + 128]
                    rhs = A[:, Voff + (sc_ * H + h_) * 64:
                            Voff + (sc_ * H + h_) * 64 + 64]
                    em("tensor", lambda t, o=PS[:, avo: avo + 64], l=lhs, r=rhs,
                       st_=(sc_ == 0), sp=(sc_ == nsc - 1): t.matmul(
                           o, l, r, start=st_, stop=sp),
                       wait=(cevs[g_] if idx == 0 else None,
                             gbank[avb] if sc_ == 0 else None),
                       inc=t_av if sc_ == nsc - 1 else None)
                gbank[("PT", slot)] = (t_av, t_av.v)

            for g_ in range(ngrp):
                emit_T_C(g_)
                if g_ >= 1:
                    emit_AV(g_ - 1)
            emit_AV(ngrp - 1)
            hpar_guard[h_ % 2] = (t_pt, t_pt.v)

        def emit_tail(qt):
            av_all = (t_av, t_av.v)
            rs = sb(o_rs, H, F32)
            rl = sb(o_rl, H, F32)
            em("vector", lambda v, o=rl, i_=rs: v.reciprocal(o, i_),
               wait=(rsg[0], rsg[1]), inc=v_e)
            src_av = PS[:, 6 * 512: 6 * 512 + H * 64].rearrange(
                "p (a x) -> p a x", x=64)
            dst = A[:, (o_atok // 2): (o_atok // 2) + C].rearrange(
                "p (a x) -> p a x", x=64)
            rlv = rl.unsqueeze(2).broadcast_to([128, H, 64])
            em("vector", lambda v, d=dst, s=src_av, r=rlv: v.tensor_tensor(
                out=d, in0=s, in1=r, op=ALU.mult), wait=(av_all,), inc=v_fin)
            fin = (v_fin, v_fin.v)
            gbank[6] = fin
            gbank[7] = fin
            for oc in range(NCC):
                inb = A[:, (o_atok // 2) + oc * 128: (o_atok // 2) + oc * 128 + 128]
                outb = bankb(4)[:, oc * 128: oc * 128 + 128]
                em("tensor", lambda t, o=outb, i_=inb: t.transpose(o, i_, ident),
                   wait=(fin if oc == 0 else None,
                         gbank[("pt", 0)] if oc == 0 else None),
                   inc=t_tr if oc == NCC - 1 else None)
            tat = (t_tr, t_tr.v)
            em("scalar", lambda sc, d=A[:, (o_atT // 2): (o_atT // 2) + C],
               s=bankb(4)[:, :C]: sc.copy(d, s), wait=(tat,), inc=s_e)
            atc = (s_e, s_e.v)
            gbank[("pt", 0)] = atc
            NOH = ceil_div(C, 512)
            WOH = min(512, C)
            for oh in range(NOH):
                bk = 6 + oh
                for cc in range(NCC):
                    lhs = A[:, (o_atT // 2) + cc * 128: (o_atT // 2) + cc * 128 + 128]
                    rhs = wprojA.rearrange("p (cc x) -> p cc x", x=C)[:, cc, oh * WOH: oh * WOH + WOH]
                    em("tensor", lambda t, o=bank(bk, WOH), l=lhs, r=rhs,
                       st_=(cc == 0), sp=(cc == NCC - 1): t.matmul(
                           o, l, r, start=st_, stop=sp),
                       wait=(atc if cc == 0 else None,
                             wproj_ready if cc == 0 else None,
                             gbank[bk] if cc == 0 else None),
                       inc=t_mm if cc == NCC - 1 else None)
            tproj = (t_mm, t_mm.v)
            x1buf = sb(o_x1A + (qt % 2) * C * 4, C, F32)
            em("vector", lambda v, o=x1buf, i_=PS[:, 6 * 512: 6 * 512 + C]:
               v.tensor_tensor(out=o, in0=i_, in1=bbc, op=ALU.add),
               wait=(tproj, c_bbc, gbank[("x1A", qt % 2)]), inc=None)
            em("vector", lambda v, o=x1buf, i_=x1buf,
               x_=sb(o_xA + (qt % 2) * C * 4, C, F32): v.tensor_tensor(
                   out=o, in0=i_, in1=x_, op=ALU.add),
               wait=(xload[qt],), inc=v_res)
            evac = (v_res, v_res.v)
            gbank[6] = evac
            gbank[7] = evac
            xbuf_guard[qt % 2] = evac
            if qt + 1 < NQT:
                em("scalar", lambda s, o=sb(o_xA + ((qt + 1) % 2) * C * 4, C, F32),
                   i_=x_in[b * T + (qt + 1) * 128: b * T + (qt + 2) * 128, :]:
                   s.dma_start(out=o, in_=i_),
                   wait=(xbuf_guard[(qt + 1) % 2],), inc=d_x, inc_n=16)
                xload[qt + 1] = (d_x, d_x.v)
            em("scalar", lambda s, o=x1_hbm[b * T + qt * 128: b * T + (qt + 1) * 128, :],
               i_=x1buf: s.dma_start(out=o, in_=i_),
               wait=(evac,), inc=d_st, inc_n=16)
            gbank[("x1A", qt % 2)] = (d_st, d_st.v)

        pend = None
        seq = [(qt, h_) for qt in range(NQT) for h_ in range(H)]
        emit_qpad(*seq[0])
        for idx, (qt, h_) in enumerate(seq):
            nmx, vred = emit_S_mm(qt, h_)
            if idx + 1 < len(seq):
                emit_qpad(*seq[idx + 1])
            if pend is not None:
                emit_PTAV(*pend)
                if pend[1] == H - 1:
                    emit_tail(pend[0])
            cev_ = emit_exps(qt, h_, nmx, vred)
            pend = (qt, h_, cev_)
        emit_PTAV(*pend)
        emit_tail(NQT - 1)
        if b == 0:
            return [xbuf_guard[0], xbuf_guard[1]]
        return (d_st, d_st.v)

    wq_guard = [None, None]
    qpad_guard = [None, None]
    attn_res_guard = [None, None]
    gbank[("x1A", 0)] = None
    gbank[("x1A", 1)] = None
    gbank[("PT", 0)] = None
    gbank[("PT", 1)] = None
    gbank[("pt", 0)] = None
    gbank[("pt", 1)] = None

    res_guard0 = attention(0)
    attn_res_guard[0] = res_guard0[0]
    attn_res_guard[1] = res_guard0[1]
    x1_stores_done = attention(1)

    w1sv = sb(o_w1s, ND, F32)
    w2sv = sb(o_w2s, ND, F32)
    h1sv = sb(o_h1s, ND, F32)
    h2sv = sb(o_h2s, ND, F32)
    w3sv = sb(o_w3s, NCC, F32)
    wfA = [sb(o_wfA + k * ND * 128 * 2, ND * 128) for k in range(2)]
    wfB = [sb(o_wfB + k * NCC * 128 * 2, NCC * 128) for k in range(2)]
    gbank.update({("wfA", 0): x1_stores_done, ("wfA", 1): x1_stores_done,
                  ("wfB", 0): x1_stores_done, ("wfB", 1): x1_stores_done})

    def get_unit(kind, oc, si):
        hbm, nin, slotset = {
            "w1": (w1B_hbm, NCC, "wfA"),
            "w2": (w2B_hbm, NCC, "wfB"),
            "h1": (h1B_hbm, ND, "wfA"),
            "h2": (h2B_hbm, ND, "wfA"),
            "w3": (w3B_hbm, ND, "wfA"),
        }[kind]
        slot = oc % 2
        dst = (wfA if slotset == "wfA" else wfB)[slot]
        width = nin * 128
        guard = gbank[(slotset, slot)]
        ln = d_t[tw_lane[0] % len(d_t)]
        tw_lane[0] += 1
        em("sync", lambda e, d=dst[:, :width], s=hbm[oc]: e.dma_start(
            out=d, in_=s, transpose=True),
           wait=(guard, wready[kind]), inc=ln, inc_n=16)
        return dst, (ln, ln.v)

    def ffn_sweep(si, t0, SW):
        NTC = SW // 128
        TH = ceil_div(SW, 512)
        WTH = min(512, SW)
        xbuf_guard = [gbank[("xF", 0)], gbank[("xF", 1)],
                      gbank[("x2F", 0)], gbank[("x2F", 1)]]
        bufs = [o_xF, o_xF + C * 4, o_x2F, o_x2F + C * 4]
        ln_tiles = emit_ln(x1_hbm, t0, NTC, bufs, xbuf_guard, o_xln2, SW)
        ln_done = ln_tiles[-1]

        def stage(kind_a, kind_b, src_off, dst_off, nin, sem_scale_a, sem_scale_b):
            pass

        units = {}
        units["w1", 0] = get_unit("w1", 0, si)
        units["w2", 0] = get_unit("w2", 0, si)
        for oc in range(ND):
            if oc + 1 < ND:
                units["w1", oc + 1] = get_unit("w1", oc + 1, si)
                units["w2", oc + 1] = get_unit("w2", oc + 1, si)
            ua, ra = units.pop(("w1", oc))
            ub, rb = units.pop(("w2", oc))
            sevs = []
            for part, (u, r) in (("a", (ua, ra)), ("b", (ub, rb))):
                for th in range(TH):
                    bk = (0 if part == "a" else 2) + th
                    for cc in range(NCC):
                        lhs = u[:, cc * 128:(cc + 1) * 128]
                        rhs = A[:, (o_xln2 // 2) + cc * SW + th * WTH:
                                (o_xln2 // 2) + cc * SW + th * WTH + WTH]
                        em("tensor", lambda t, o=bank(bk, WTH), l=lhs, r=rhs,
                           st_=(cc == 0), sp=(cc == NCC - 1): t.matmul(
                               o, l, r, start=st_, stop=sp),
                           wait=(r if cc == 0 else None,
                                 ln_tiles[min(th * 4 + 3, NTC - 1)] if cc == 0 else None,
                                 gbank[bk] if cc == 0 else None),
                           inc=t_mm if cc == NCC - 1 else None)
                    sevs.append(((t_mm, t_mm.v), bk, th))
            gbank[("wfA", oc % 2)] = (t_mm, t_mm.v)
            gbank[("wfB", oc % 2)] = (t_mm, t_mm.v)
            for th in range(TH):
                tma = sevs[th][0]
                bka = sevs[th][1]
                tmb = sevs[TH + th][0]
                bkb = sevs[TH + th][1]
                sl = A[:, (o_silu // 2) + th * WTH: (o_silu // 2) + th * WTH + WTH]
                em("scalar", lambda sc, o=sl, i_=bank(bka, WTH), s1=w1sv[:, oc:oc + 1]:
                   sc.activation(out=o, in_=i_, func=AF.Silu, scale=s1),
                   wait=(tma, gbank[("silu", th)]), inc=s_e)
                sev = (s_e, s_e.v)
                gbank[bka] = sev
                hdst = A[:, (o_h // 2) + oc * SW + th * WTH:
                         (o_h // 2) + oc * SW + th * WTH + WTH]
                em("vector", lambda v, o=hdst, i_=bank(bkb, WTH),
                   s1=w2sv[:, oc:oc + 1], s2=sl: v.scalar_tensor_tensor(
                       out=o, in0=i_, scalar=s1, in1=s2, op0=ALU.mult, op1=ALU.mult),
                   wait=(tmb, sev), inc=v_e)
                gbank[bkb] = (v_e, v_e.v)
                gbank[("silu", th)] = (v_e, v_e.v)
        h_done = (v_e, v_e.v)

        def gstage(kind, src_o, dst_o, scalev, actdone_prev):
            units2 = {0: get_unit(kind, 0, si)}
            out_ev = None
            for oc in range(ND):
                if oc + 1 < ND:
                    units2[oc + 1] = get_unit(kind, oc + 1, si)
                u, r = units2.pop(oc)
                tms = []
                for th in range(TH):
                    bk = (oc % 2) * 2 + th
                    for dc in range(ND):
                        lhs = u[:, dc * 128:(dc + 1) * 128]
                        rhs = A[:, (src_o // 2) + dc * SW + th * WTH:
                                (src_o // 2) + dc * SW + th * WTH + WTH]
                        em("tensor", lambda t, o=bank(bk, WTH), l=lhs, r=rhs,
                           st_=(dc == 0), sp=(dc == ND - 1): t.matmul(
                               o, l, r, start=st_, stop=sp),
                           wait=(r if dc == 0 else None,
                                 actdone_prev if dc == 0 else None,
                                 gbank[bk] if dc == 0 else None),
                           inc=t_mm if dc == ND - 1 else None)
                    tms.append(((t_mm, t_mm.v), bk))
                gbank[("wfA", oc % 2)] = (t_mm, t_mm.v)
                for th in range(TH):
                    tm, bk = tms[th]
                    dst = A[:, (dst_o // 2) + oc * SW + th * WTH:
                            (dst_o // 2) + oc * SW + th * WTH + WTH]
                    em("scalar", lambda sc, o=dst, i_=bank(bk, WTH),
                       s1=scalev[:, oc:oc + 1]: sc.activation(
                           out=o, in_=i_, func=AF.Gelu, scale=s1),
                       wait=(tm,), inc=s_e)
                    gbank[bk] = (s_e, s_e.v)
                    out_ev = (s_e, s_e.v)
            return out_ev

        g1_done = gstage("h1", o_h, o_g, h1sv, h_done)
        g2_done = gstage("h2", o_g, o_h, h2sv, g1_done)

        yreg = o_xln2
        units3 = {0: get_unit("w3", 0, si)}
        for half in range(TH):
            for oc in range(NCC):
                if (half, oc) != (TH - 1, NCC - 1):
                    nxt = oc + 1 if oc + 1 < NCC else 0
                    if nxt != 0 or half + 1 < TH:
                        if nxt not in units3:
                            units3[nxt] = get_unit("w3", nxt, si)
                u, r = units3.pop(oc) if oc in units3 else get_unit("w3", oc, si)
                bk = (oc % 2) * 2
                for dc in range(ND):
                    lhs = u[:, dc * 128:(dc + 1) * 128]
                    rhs = A[:, (o_h // 2) + dc * SW + half * WTH:
                            (o_h // 2) + dc * SW + half * WTH + WTH]
                    em("tensor", lambda t, o=bank(bk, WTH), l=lhs, r=rhs,
                       st_=(dc == 0), sp=(dc == ND - 1): t.matmul(
                           o, l, r, start=st_, stop=sp),
                       wait=(r if dc == 0 else None,
                             g2_done if dc == 0 else None,
                             gbank[bk] if dc == 0 else None),
                       inc=t_mm if dc == ND - 1 else None)
                tm = (t_mm, t_mm.v)
                gbank[("wfA", oc % 2)] = (t_mm, t_mm.v)
                ydst = sb(yreg + oc * WTH * 4, WTH, F32)
                em("scalar", lambda sc, o=ydst, i_=bank(bk, WTH),
                   s1=w3sv[:, oc:oc + 1]: sc.activation(
                       out=o, in_=i_, func=AF.Identity, scale=s1),
                   wait=(tm, gbank[("y", oc)]), inc=s_e)
                gbank[bk] = (s_e, s_e.v)
                gbank[("y", oc)] = (s_e, s_e.v)
            yev = (s_e, s_e.v)
            for tci in range(WTH // 128):
                tc = half * (WTH // 128) + tci
                xb = sb(o_xF + (tc % 2) * C * 4, C, F32)
                em("scalar", lambda s, t_=t0 + tc * 128, b_=xb: s.dma_start(
                    out=b_, in_=x1_hbm[t_: t_ + 128, :]),
                   wait=(xbuf_guard[tc % 2],), inc=d_x, inc_n=16)
                xld = (d_x, d_x.v)
                for oc in range(NCC):
                    inb = sb(yreg + oc * WTH * 4 + tci * 128 * 4, 128, F32)
                    outb = PS[:, (4 + oc // 4) * 512 + (oc % 4) * 128:
                              (4 + oc // 4) * 512 + (oc % 4) * 128 + 128]
                    em("tensor", lambda t, o=outb, i_=inb: t.transpose(o, i_, identf),
                       wait=(yev if oc == 0 else None, c_bbc if oc == 0 else None,
                             gbank[("yT",)] if oc == 0 else None),
                       inc=t_tr if oc == NCC - 1 else None)
                ttr = (t_tr, t_tr.v)
                x2b = sb(o_x2F + (tc % 2) * C * 4, C, F32)
                em("vector", lambda v, o=x2b, i_=PS[:, 4 * 512: 4 * 512 + C], x_=xb:
                   v.tensor_tensor(out=o, in0=i_, in1=x_, op=ALU.add),
                   wait=(ttr, xld, gbank[("x2F", tc % 2)]), inc=v_res)
                ev = (v_res, v_res.v)
                gbank[("yT",)] = ev
                xbuf_guard[tc % 2] = ev
                em("scalar", lambda s, t_=t0 + tc * 128, b_=x2b: s.dma_start(
                    out=out_hbm[t_: t_ + 128, :], in_=b_),
                   wait=(ev,), inc=d_st, inc_n=16)
                gbank[("x2F", tc % 2)] = (d_st, d_st.v)
            for oc in range(NCC):
                gbank[("y", oc)] = (v_res, v_res.v)
        gbank[("xF", 0)] = xbuf_guard[0]
        gbank[("xF", 1)] = xbuf_guard[1]

    gbank.update({("silu", 0): None, ("silu", 1): None,
                  ("xF", 0): (v_res, 0), ("xF", 1): None,
                  ("x2F", 0): None, ("x2F", 1): None, ("yT",): None})
    for oc in range(NCC):
        gbank[("y", oc)] = None
    gbank[("xF", 0)] = x1_stores_done
    gbank[("xF", 1)] = x1_stores_done
    gbank[("x2F", 0)] = x1_stores_done
    gbank[("x2F", 1)] = x1_stores_done

    t0 = 0
    for si, SW in enumerate(sweeps):
        ffn_sweep(si, t0, SW)
        t0 += SW

    final_st = d_st.v
    em("sync", lambda s: s.wait_ge(d_st.h, final_st))

    with nc.Block() as block:
        def run(eng_name):
            def _(eng):
                for waits, fn, inc, inc_n in prog.q[eng_name]:
                    for s, v in waits:
                        eng.wait_ge(s.h, v)
                    inst = fn(eng)
                    if inc is not None:
                        inst.then_inc(inc.h, inc_n)
            return _
        block.tensor(run("tensor"))
        block.vector(run("vector"))
        block.scalar(run("scalar"))
        block.sync(run("sync"))
        block.gpsimd(run("gpsimd"))

    return nc, st, prog


def make_inputs(cfg, full_inputs):
    B, T, C, H = cfg["B"], cfg["T"], cfg["C"], cfg["H"]
    n_cores = 16 // B if False else None
    x = full_inputs["L0_attn_x"]
    ncores = x.shape[0] // B
    ident = np.eye(128, dtype=np.float32)
    binT = np.triu(np.ones((128, 128), dtype=np.float32), k=1) * -100000.0
    bproj_bc = np.ascontiguousarray(
        np.broadcast_to(np.asarray(full_inputs["L0_attn_b_proj"], np.float32), (128, C)))
    common = dict(
        identf=ident,
        wqkv=np.ascontiguousarray(full_inputs["L0_attn_w_qkv"], np.float32),
        wproj=np.ascontiguousarray(full_inputs["L0_attn_w_proj"], np.float32),
        bproj_bc=bproj_bc,
        w1=np.ascontiguousarray(full_inputs["L0_ffn_w1_w"], np.float32),
        w2=np.ascontiguousarray(full_inputs["L0_ffn_w2_w"], np.float32),
        h1=np.ascontiguousarray(full_inputs["L0_ffn_h1_w"], np.float32),
        h2=np.ascontiguousarray(full_inputs["L0_ffn_h2_w"], np.float32),
        w3=np.ascontiguousarray(full_inputs["L0_ffn_w3_w"], np.float32),
        w1s=np.ascontiguousarray(
            np.asarray(full_inputs["L0_ffn_w1_s"], np.float32).reshape(-1, 128).T),
        w2s=np.ascontiguousarray(
            np.asarray(full_inputs["L0_ffn_w2_s"], np.float32).reshape(-1, 128).T),
        h1s=np.ascontiguousarray(
            np.asarray(full_inputs["L0_ffn_h1_s"], np.float32).reshape(-1, 128).T),
        h2s=np.ascontiguousarray(
            np.asarray(full_inputs["L0_ffn_h2_s"], np.float32).reshape(-1, 128).T),
        w3s=np.ascontiguousarray(
            np.asarray(full_inputs["L0_ffn_w3_s"], np.float32).reshape(-1, 128).T),
        ident=ident, binT=binT,
    )
    in_maps = []
    for c in range(ncores):
        m = dict(common)
        m["x"] = np.ascontiguousarray(
            np.asarray(x[c * B:(c + 1) * B], np.float32).reshape(B * T, C))
        in_maps.append(m)
    return in_maps


def kernel(L0_attn_x, L0_attn_w_qkv, L0_attn_w_proj, L0_attn_b_proj,
           L0_ffn_w1_w, L0_ffn_w1_s, L0_ffn_w2_w, L0_ffn_w2_s,
           L0_ffn_h1_w, L0_ffn_h1_s, L0_ffn_h2_w, L0_ffn_h2_s,
           L0_ffn_w3_w, L0_ffn_w3_s):
    cfg = real_cfg()
    full = dict(L0_attn_x=L0_attn_x, L0_attn_w_qkv=L0_attn_w_qkv,
                L0_attn_w_proj=L0_attn_w_proj, L0_attn_b_proj=L0_attn_b_proj,
                L0_ffn_w1_w=L0_ffn_w1_w, L0_ffn_w1_s=L0_ffn_w1_s,
                L0_ffn_w2_w=L0_ffn_w2_w, L0_ffn_w2_s=L0_ffn_w2_s,
                L0_ffn_h1_w=L0_ffn_h1_w, L0_ffn_h1_s=L0_ffn_h1_s,
                L0_ffn_h2_w=L0_ffn_h2_w, L0_ffn_h2_s=L0_ffn_h2_s,
                L0_ffn_w3_w=L0_ffn_w3_w, L0_ffn_w3_s=L0_ffn_w3_s)
    in_maps = make_inputs(cfg, full)
    nc, stack, _prog = build(cfg)
    try:
        res = run_bass_kernel_spmd(nc, in_maps, core_ids=list(range(8)))
    finally:
        stack.close()
    B, T, C = cfg["B"], cfg["T"], cfg["C"]
    outs = [r["out"].reshape(B, T, C) for r in res.results]
    return np.concatenate(outs, axis=0)


# version 22
# speedup vs baseline: 1.0765x; 1.0765x over previous
import math
from contextlib import ExitStack

import numpy as np
import ml_dtypes

import concourse.bass as bass
import concourse.mybir as mybir
from concourse.bass_utils import run_bass_kernel_spmd

F32 = mybir.dt.float32
BF16 = mybir.dt.bfloat16
AF = mybir.ActivationFunctionType
ALU = mybir.AluOpType
AX = mybir.AxisListType

BIG = 3.0e38
ENGINES = ("tensor", "vector", "scalar", "sync", "gpsimd")


def real_cfg():
    return dict(B=2, T=2048, C=1024, H=16, HS=64, DH=4096,
                sweeps=(1024, 1024, 1024, 1024))


class Sem:
    __slots__ = ("h", "v", "name")

    def __init__(self, h, name):
        self.h = h
        self.v = 0
        self.name = name


class Prog:
    def __init__(self):
        self.q = {e: [] for e in ENGINES}
        self.waited = {e: {} for e in ENGINES}
        self.self_sem = {}

    def emit(self, eng, fn, wait=(), inc=None, inc_n=1):
        wait = list(wait)
        if inc is not None and inc_n >= 16 and inc.v > 0:
            wait.append((inc, inc.v))
        ss = self.self_sem.get(eng)
        if ss is not None and not (inc is not None and inc_n >= 16):
            assert inc is None or inc is ss, f"{eng} must inc its engine sem"
            if ss.v > 0:
                wait.append((ss, ss.v))
            inc, inc_n = ss, 1
        w = []
        for item in wait:
            if item is None:
                continue
            s, v = item
            if v <= 0:
                continue
            seen = self.waited[eng].get(id(s), 0)
            if v > seen:
                self.waited[eng][id(s)] = v
                w.append((s, v))
        tgt = None
        if inc is not None:
            inc.v += inc_n
            tgt = inc.v
        self.q[eng].append((tuple(w), fn, inc, inc_n))
        return (inc, tgt) if inc is not None else None


def ceil_div(a, b):
    return (a + b - 1) // b


def build(cfg):
    B, T, C, H, HS, DH = (cfg[k] for k in ("B", "T", "C", "H", "HS", "DH"))
    sweeps = cfg["sweeps"]
    assert HS == 64 and H % 2 == 0 and H * HS == C
    assert C % 128 == 0 and T % 128 == 0 and DH % 128 == 0
    assert sum(sweeps) == B * T and all(s % 128 == 0 for s in sweeps)
    NCC = C // 128
    NQT = T // 128
    NHP = H // 2
    TOK = B * T
    ND = DH // 128
    SWMAX = max(sweeps)
    NBAV = ceil_div(H, 7)
    AVB = 8 - NBAV
    nstat = ceil_div(C, 512)

    nc = bass.Bass("TRN2", target_bir_lowering=False)

    dt = nc.dram_tensor
    x_in = dt("x", [TOK, C], F32, kind="ExternalInput").ap()
    wqkv = dt("wqkv", [H, 3 * HS, C], F32, kind="ExternalInput").ap()
    wproj = dt("wproj", [C, C], F32, kind="ExternalInput").ap()
    bproj_bc = dt("bproj_bc", [128, C], F32, kind="ExternalInput").ap()
    w1 = dt("w1", [DH, C], F32, kind="ExternalInput").ap()
    w2 = dt("w2", [DH, C], F32, kind="ExternalInput").ap()
    h1 = dt("h1", [DH, DH], F32, kind="ExternalInput").ap()
    h2 = dt("h2", [DH, DH], F32, kind="ExternalInput").ap()
    w3 = dt("w3", [C, DH], F32, kind="ExternalInput").ap()
    w1s = dt("w1s", [128, ND], F32, kind="ExternalInput").ap()
    w2s = dt("w2s", [128, ND], F32, kind="ExternalInput").ap()
    h1s = dt("h1s", [128, ND], F32, kind="ExternalInput").ap()
    h2s = dt("h2s", [128, ND], F32, kind="ExternalInput").ap()
    w3s = dt("w3s", [128, NCC], F32, kind="ExternalInput").ap()
    ident_in = dt("ident", [128, 128], F32, kind="ExternalInput").ap()
    binT_in = dt("binT", [128, 128], F32, kind="ExternalInput").ap()
    identf_in = dt("identf", [128, 128], F32, kind="ExternalInput").ap()
    out_hbm = dt("out", [TOK, C], F32, kind="ExternalOutput").ap()

    x1_hbm = dt("x1_scratch", [TOK, C], F32, kind="Internal").ap()
    qkvB_hbm = dt("qkvB", [NHP, NCC * 384, 128], BF16, kind="Internal").ap()
    projB_hbm = dt("projB", [NCC, NCC * 128, 128], BF16, kind="Internal").ap()
    w1B_hbm = dt("w1B", [ND, NCC * 128, 128], BF16, kind="Internal").ap()
    w2B_hbm = dt("w2B", [ND, NCC * 128, 128], BF16, kind="Internal").ap()
    h1B_hbm = dt("h1B", [ND, ND * 128, 128], BF16, kind="Internal").ap()
    h2B_hbm = dt("h2B", [ND, ND * 128, 128], BF16, kind="Internal").ap()
    w3B_hbm = dt("w3B", [NCC, ND * 128, 128], BF16, kind="Internal").ap()

    st = ExitStack()
    class Alloc:
        def __init__(self):
            self.off = 0

        def take(self, nbytes, align=4):
            self.off = (self.off + align - 1) // align * align
            o = self.off
            self.off += nbytes
            return o

    al = Alloc()
    o_ident = al.take(128 * 2)
    o_binT = al.take(128 * 2)
    o_identf = al.take(128 * 4)
    o_bbc = al.take(C * 4)
    o_w1s = al.take(ND * 4)
    o_w2s = al.take(ND * 4)
    o_h1s = al.take(ND * 4)
    o_h2s = al.take(ND * 4)
    o_w3s = al.take(NCC * 4)
    o_stats = al.take(4 * 12 * 4)
    o_mv = al.take(4 * 2 * 4)
    o_std = al.take(4 * 4)
    o_istd = al.take(4 * 4)
    o_lntok = al.take(2 * C * 2)
    o_nmp = al.take(2 * 4 * 4)
    o_rsp = al.take(2 * 4 * 4)
    o_rs = al.take(H * 4)
    o_rl = al.take(H * 4)
    p0 = al.off
    al.off = p0
    o_xlnA = al.take(NCC * T * 2)
    o_Q = al.take(NHP * T * 2)
    o_K = al.take(NHP * T * 2)
    o_V = al.take((T // 128) * H * 64 * 2)
    o_P = al.take(2 * T * 2)
    o_PT = al.take(2 * 512 * 2)
    o_qpad = al.take(2 * 128 * 2)
    o_atok = al.take(C * 2)
    o_atT = al.take(C * 2)
    o_wqkvA = al.take(2 * NCC * 384 * 2, align=32)
    o_wprojA = al.take(NCC * C * 2, align=32)
    o_xA = al.take(2 * C * 4)
    o_x1A = al.take(2 * C * 4)
    endA = al.off
    al.off = p0
    o_xln2 = al.take(NCC * SWMAX * 2)
    o_h = al.take(ND * SWMAX * 2)
    o_g = al.take(ND * SWMAX * 2)
    o_wfA = al.take(2 * ND * 128 * 2, align=32)
    o_wfB = al.take(2 * NCC * 128 * 2, align=32)
    o_silu = al.take(SWMAX * 2)
    o_xF = al.take(2 * C * 4)
    o_x2F = al.take(2 * C * 4)
    endF = al.off
    total = max(endA, endF)
    arena_elems = ceil_div(total, 2) + 64

    arena = st.enter_context(nc.sbuf_tensor("arena", [128, arena_elems], BF16))
    ps_t = st.enter_context(nc.psum_tensor("ps", [128, 4096], F32))
    A = arena.ap()
    PS = ps_t.ap()

    def sb(off, n, dtype=BF16, parts=(0, 128)):
        if dtype == BF16:
            v = A[parts[0]:parts[1], off // 2: off // 2 + n]
        else:
            v = A[parts[0]:parts[1], off // 2: off // 2 + 2 * n].bitcast(F32)
        return v

    def bank(i, w=512):
        return PS[:, i * 512: i * 512 + w]

    def bankb(i, w=1024):
        return PS[:, i * 512: (i + 1) * 512].bitcast(BF16)[:, :w]

    ident = sb(o_ident, 128)
    mtri = sb(o_binT, 128)
    identf = sb(o_identf, 128, F32)
    bbc = sb(o_bbc, C, F32)

    prog = Prog()
    sems = []

    def mksem(name):
        s = Sem(st.enter_context(nc.semaphore(name)), name)
        sems.append(s)
        return s

    d_x = mksem("d_x")
    d_x2 = mksem("d_x2")
    d_st = mksem("d_st")
    d_w = mksem("d_w")
    d_t = [mksem(f"d_t{i}") for i in range(6)]
    t_mm = mksem("t_mm")
    t_s = mksem("t_s")
    t_pt = mksem("t_pt")
    t_av = mksem("t_av")
    t_tr = mksem("t_tr")
    v_e = mksem("v")
    v_red = v_fin = v_res = v_e
    s_e = mksem("s")
    s_exp = s_wt = s_e
    prog.self_sem = {"vector": v_e, "scalar": s_e}

    gbank = {i: None for i in range(8)}
    tw_lane = [0, 0]

    def em(eng, fn, wait=(), inc=None, inc_n=1):
        return prog.emit(eng, fn, wait, inc, inc_n)

    em("gpsimd", lambda g: g.dma_start(out=ident, in_=ident_in), inc=d_w, inc_n=16)
    em("gpsimd", lambda g: g.dma_start(out=mtri, in_=binT_in), inc=d_w, inc_n=16)
    em("scalar", lambda s: s.dma_start(out=identf, in_=identf_in), inc=d_x, inc_n=16)
    em("scalar", lambda s: s.dma_start(out=bbc, in_=bproj_bc), inc=d_x, inc_n=16)
    for name, off, src, n in (("w1s", o_w1s, w1s, ND), ("w2s", o_w2s, w2s, ND),
                              ("h1s", o_h1s, h1s, ND), ("h2s", o_h2s, h2s, ND),
                              ("w3s", o_w3s, w3s, NCC)):
        dstv = sb(off, n, F32)
        em("scalar", lambda s, d=dstv, sv=src: s.dma_start(out=d, in_=sv),
           inc=d_x, inc_n=16)
    c_after_const = (d_w, d_w.v)
    c_bbc = (d_x, d_x.v)

    wqkvA = [sb(o_wqkvA + k * NCC * 384 * 2, NCC * 384) for k in range(2)]
    wprojA = sb(o_wprojA, NCC * C)

    qkv_cast = [None] * NHP
    for hp in range(NHP):
        for half in range(2):
            s_ap = wqkv[2 * hp + half].rearrange(
                "(part r) (cc c) -> cc part r c", r=HS, c=128)
            d_ap = qkvB_hbm[hp].rearrange(
                "(cc part half r) c -> cc part half r c", part=3, half=2, r=HS)[
                :, :, half]
            em("gpsimd", lambda g, d=d_ap, s_=s_ap: g.dma_start(out=d, in_=s_),
               inc=d_w, inc_n=16)
        qkv_cast[hp] = (d_w, d_w.v)
    for oc in range(NCC):
        s_ap = wproj[oc * 128:(oc + 1) * 128, :].rearrange(
            "r (cc c) -> cc r c", c=128)
        d_ap = projB_hbm[:, oc * 128:(oc + 1) * 128, :]
        em("gpsimd", lambda g, d=d_ap, s_=s_ap: g.dma_start(out=d, in_=s_),
           inc=d_w, inc_n=16)
    proj_cast = (d_w, d_w.v)
    em("sync", lambda e: e.dma_start(
        out=wprojA, in_=projB_hbm.rearrange("a b c -> (a b) c"), transpose=True),
       wait=(proj_cast,), inc=d_t[0], inc_n=16)
    wproj_ready = (d_t[0], d_t[0].v)

    wready = {}
    for kind, src, dst, nin, nout in (("w1", w1, w1B_hbm, NCC, ND),
                                      ("w2", w2, w2B_hbm, NCC, ND),
                                      ("h1", h1, h1B_hbm, ND, ND),
                                      ("h2", h2, h2B_hbm, ND, ND),
                                      ("w3", w3, w3B_hbm, ND, NCC)):
        for cc in range(nin):
            s_ap = src[:, cc * 128:(cc + 1) * 128].rearrange(
                "(oc r) c -> oc r c", r=128)
            d_ap = dst[:, cc * 128:(cc + 1) * 128, :]
            em("gpsimd", lambda g, d=d_ap, s_=s_ap: g.dma_start(out=d, in_=s_),
               inc=d_w, inc_n=16)
        wready[kind] = (d_w, d_w.v)

    stats4 = [sb(o_stats + k * 12 * 4, 12, F32) for k in range(4)]
    mvs4 = [sb(o_mv + k * 2 * 4, 2, F32) for k in range(4)]
    vars4 = sb(o_mv, 8, F32).rearrange("p (a b) -> p a b", b=2)[:, :, 1:2]
    stdv = sb(o_std, 4, F32)
    istdv = sb(o_istd, 4, F32)

    def emit_ln(src_hbm, row0, ntiles, bufs, xbuf_guard, dst_off, dst_T,
                tbanks=(4, 5)):
        nb = len(bufs)
        lanes_x = (d_x, d_x2)

        def load(i):
            buf = sb(bufs[i % nb], C, F32)
            ln_ = lanes_x[i % 2]
            em("scalar", lambda s, b=buf: s.dma_start(
                out=b, in_=src_hbm[row0 + i * 128: row0 + (i + 1) * 128, :]),
               wait=(xbuf_guard[i % nb],), inc=ln_, inc_n=16)
            return (ln_, ln_.v)

        tile_done = []
        last_ttr = [None, None]
        for g in range(ceil_div(ntiles, 4)):
            tiles = list(range(g * 4, min(g * 4 + 4, ntiles)))
            gn = len(tiles)
            sevs = {}
            for i in tiles:
                ld = load(i)
                buf = sb(bufs[i % nb], C, F32)
                for k in range(nstat):
                    w = min(512, C - k * 512)
                    em("vector", lambda v, st_=stats4[i % 4][:, k * 6:(k + 1) * 6],
                       b=buf[:, k * 512: k * 512 + w]: v.bn_stats(st_, b),
                       wait=(ld,), inc=v_e)
                sevs[i] = (v_e, v_e.v)
            for i in tiles:
                em("vector", lambda v, m=mvs4[i % 4],
                   st_=stats4[i % 4][:, :6 * nstat]: v.bn_aggr(m, st_),
                   wait=(sevs[i],), inc=v_e)
            aggr_ev = (v_e, v_e.v)
            em("vector", lambda v: v.tensor_scalar(
                out=vars4[:, 0:gn], in0=vars4[:, 0:gn], scalar1=1e-5,
                scalar2=None, op0=ALU.add), wait=(aggr_ev,), inc=v_e)
            ve = (v_e, v_e.v)
            em("scalar", lambda sc, o=stdv[:, 0:gn], i_=vars4[:, 0:gn]:
               sc.activation(out=o, in_=i_, func=AF.Sqrt), wait=(ve,), inc=s_e)
            se = (s_e, s_e.v)
            em("vector", lambda v, o=istdv[:, 0:gn], i_=stdv[:, 0:gn]:
               v.reciprocal(o, i_), wait=(se,), inc=v_e)
            for i in tiles:
                buf = sb(bufs[i % nb], C, F32)
                xlntok = sb(o_lntok + (i % 2) * C * 2, C)
                em("vector", lambda v, o=xlntok, b=buf, m=mvs4[i % 4],
                   i_=istdv[:, (i % 4):(i % 4) + 1]:
                   v.tensor_scalar(out=o, in0=b, scalar1=m[:, 0:1], scalar2=i_,
                                   op0=ALU.subtract, op1=ALU.mult),
                   wait=(gbank[tbanks[i % 2]], last_ttr[i % 2]), inc=v_e)
                xbuf_guard[i % nb] = (v_e, v_e.v)
                applied = (v_e, v_e.v)
                bk = tbanks[i % 2]
                for cc in range(NCC):
                    inb = xlntok[:, cc * 128:(cc + 1) * 128]
                    outb = bankb(bk)[:, cc * 128:(cc + 1) * 128]
                    em("tensor", lambda t, o=outb, i_=inb: t.transpose(o, i_, ident),
                       wait=(applied if cc == 0 else None,
                             c_after_const if cc == 0 else None,
                             gbank[bk] if cc == 0 else None),
                       inc=t_tr if cc == NCC - 1 else None)
                ttr = (t_tr, t_tr.v)
                last_ttr[i % 2] = ttr
                dstv = A[:, dst_off // 2: dst_off // 2 + NCC * dst_T].rearrange(
                    "p (cc x) -> p cc x", x=dst_T)[:, :, i * 128:(i + 1) * 128]
                srcv = bankb(bk)[:, :NCC * 128].rearrange("p (cc x) -> p cc x", x=128)
                em("scalar", lambda sc, d=dstv, s=srcv: sc.copy(d, s),
                   wait=(ttr,), inc=s_e)
                gbank[bk] = (s_e, s_e.v)
                tile_done.append((s_e, s_e.v))
        return tile_done

    xlnA_ready = None

    def attention(b):
        nonlocal xlnA_ready
        if b == 0:
            xbuf_guard = [None, None, None, None]
        else:
            xbuf_guard = [attn_res_guard[0], attn_res_guard[1],
                          gbank[("x1A", 0)], gbank[("x1A", 1)]]
        bufs = [o_xA, o_xA + C * 4, o_x1A, o_x1A + C * 4]
        ln_tiles = emit_ln(x_in, b * T, NQT, bufs, xbuf_guard, o_xlnA, T)
        ln_done = ln_tiles[-1]
        xlnA_ready = ln_done

        vavprev = (t_av, t_av.v)
        Voff = o_V // 2
        NT5 = T // 512 if T >= 512 else 1
        W5 = min(512, T)
        for hp in range(NHP):
            slot = hp % 2
            unit = wqkvA[slot]
            ln_q = d_t[tw_lane[0] % len(d_t)]
            tw_lane[0] += 1
            em("sync", lambda e, u=unit, hp_=hp: e.dma_start(
                out=u, in_=qkvB_hbm[hp_], transpose=True),
               wait=(qkv_cast[hp], wq_guard[slot]), inc=ln_q, inc_n=16)
            unit_rdy = (ln_q, ln_q.v)
            uv = unit.rearrange("p (cc x) -> p cc x", x=384)
            for part, dsto, evac_eng in ((0, o_Q, "scalar"), (1, o_K, "vector")):
                for tt in range(NT5):
                    bk = part * 2 + (tt % 2) if False else (0 if part == 0 else 1)
                    bk = (tt % 2) + (0 if part == 0 else 2)
                    for cc in range(NCC):
                        lhs = uv[:, cc, part * 128:(part + 1) * 128]
                        rhs = A[:, (o_xlnA // 2) + cc * T + tt * W5:
                                (o_xlnA // 2) + cc * T + tt * W5 + W5]
                        em("tensor", lambda t, o=bank(bk, W5), l=lhs, r=rhs,
                           st_=(cc == 0), sp=(cc == NCC - 1): t.matmul(
                               o, l, r, start=st_, stop=sp),
                           wait=(unit_rdy if cc == 0 else None,
                                 xlnA_ready if cc == 0 else None,
                                 gbank[bk] if cc == 0 else None),
                           inc=t_mm if cc == NCC - 1 else None)
                    tm = (t_mm, t_mm.v)
                    dst = A[:, (dsto // 2) + hp * T + tt * W5:
                            (dsto // 2) + hp * T + tt * W5 + W5]
                    if part == 0:
                        em("scalar", lambda sc, d=dst, s=bank(bk, W5):
                           sc.mul(d, s, 0.125), wait=(tm,), inc=s_e)
                        gbank[bk] = (s_e, s_e.v)
                    else:
                        em("vector", lambda v, d=dst, s=bank(bk, W5):
                           v.tensor_copy(d, s), wait=(tm,), inc=v_e)
                        gbank[bk] = (v_e, v_e.v)
            for sc_ in range(T // 128):
                bk = 4 + (sc_ % 2)
                for cc in range(NCC):
                    lhs = A[:, (o_xlnA // 2) + cc * T + sc_ * 128:
                            (o_xlnA // 2) + cc * T + (sc_) * 128 + 128]
                    rhs = uv[:, cc, 256:384]
                    em("tensor", lambda t, o=bank(bk, 128), l=lhs, r=rhs,
                       st_=(cc == 0), sp=(cc == NCC - 1): t.matmul(
                           o, l, r, start=st_, stop=sp),
                       wait=(unit_rdy if cc == 0 else None,
                             xlnA_ready if cc == 0 else None,
                             gbank[bk] if cc == 0 else None),
                       inc=t_mm if cc == NCC - 1 else None)
                tm = (t_mm, t_mm.v)
                dst = A[:, Voff + (sc_ * H + 2 * hp) * 64:
                        Voff + (sc_ * H + 2 * hp) * 64 + 128]
                em("vector", lambda v, d=dst, s=bank(bk, 128): v.tensor_copy(d, s),
                   wait=(tm, vavprev), inc=v_e)
                gbank[bk] = (v_e, v_e.v)
            wq_guard[slot] = (t_mm, t_mm.v)
        qkv_done_t = (t_mm, t_mm.v)
        qkv_done_v = (v_e, v_e.v)
        qkv_done_s = (s_e, s_e.v)

        em("scalar", lambda s: s.dma_start(
            out=sb(o_xA, C, F32), in_=x_in[b * T: b * T + 128, :]),
           wait=(xbuf_guard[0],), inc=d_x, inc_n=16)
        xload = {0: (d_x, d_x.v)}
        hpar_guard = [None, None]
        expg = [None, None]
        rsg = [None, None]
        if b == 0:
            em("vector", lambda v: v.memset(
                A[:, o_qpad // 2: o_qpad // 2 + 256], 0.0), inc=v_e)
            qpad_init = (v_e, v_e.v)
            qpad_guard[0] = qpad_guard[1] = qpad_init
        qpad_evs = [None, None]

        def emit_qpad(qt, h_):
            par = h_ % 2
            lo = par * 64
            hpv = (h_ // 2) * T
            qp = (o_qpad // 2) + par * 128
            em("scalar", lambda sc,
               d=A[lo:lo + 64, qp: qp + 128],
               s=A[lo:lo + 64, (o_Q // 2) + hpv + qt * 128:
                   (o_Q // 2) + hpv + qt * 128 + 128]: sc.copy(d, s),
               wait=(qpad_guard[par],), inc=s_e)
            qpad_evs[par] = (s_e, s_e.v)

        def emit_S_mm(qt, h_):
            K_ = (qt + 1) * 128
            npiece = ceil_div(K_, 1024)
            qtw = qt * 128
            pd = qtw // 1024
            hp, lo = h_ // 2, (h_ % 2) * 64
            hpv = hp * T
            par = h_ % 2
            nm = sb(o_nmp + par * 16, 4, F32)
            qp = (o_qpad // 2) + par * 128
            qpad_ev = qpad_evs[par]
            sbase = par * 1024 if npiece == 1 else 0
            ts_p = []
            for p in range(npiece):
                w_p = min(1024, K_ - p * 1024)
                for jj in range(ceil_div(w_p, 512)):
                    w_j = min(512, w_p - jj * 512)
                    bk = sbase // 512 + 2 * p + jj
                    off = p * 1024 + jj * 512
                    lhs = A[:, qp: qp + 128]
                    rhs = A[:, (o_K // 2) + hpv + off:
                            (o_K // 2) + hpv + off + w_j]
                    em("tensor", lambda t, o=bank(bk, w_j), l=lhs, r=rhs:
                       t.matmul(o, l, r, start=True, stop=True),
                       wait=(gbank[bk], qpad_ev,
                             qkv_done_v if qt == 0 and h_ == 0 else None,
                             qkv_done_s if qt == 0 and h_ == 0 else None),
                       inc=t_s if jj == ceil_div(w_p, 512) - 1 else None)
                ts_p.append((t_s, t_s.v))
                if p < pd:
                    em("vector", lambda v, o=nm[:, p:p + 1],
                       i_=PS[:, sbase + p * 1024: sbase + p * 1024 + 1024]:
                       v.tensor_reduce(out=o, in_=i_, axis=AX.X, op=ALU.max,
                                       negate=True),
                       wait=(ts_p[p], expg[h_ % 2]))
            qpad_guard[par] = (t_s, t_s.v)
            em("vector", lambda v, o=PS[:, sbase + qtw: sbase + qtw + 128], m=mtri:
               v.tensor_tensor(out=o, in0=o, in1=m, op=ALU.add),
               wait=(ts_p[pd], expg[h_ % 2]))
            w_pd = K_ - pd * 1024
            em("vector", lambda v, o=nm[:, pd:pd + 1],
               i_=PS[:, sbase + pd * 1024: sbase + pd * 1024 + w_pd]:
               v.tensor_reduce(out=o, in_=i_, axis=AX.X, op=ALU.max,
                               negate=True))
            if npiece > 1:
                em("vector", lambda v, o=nm[:, 3:4], i_=nm[:, 0:npiece]:
                   v.tensor_reduce(out=o, in_=i_, axis=AX.X, op=ALU.min,
                                   negate=False))
            nmx = nm[:, 3:4] if npiece > 1 else nm[:, 0:1]
            return nmx, (v_e, v_e.v)

        def emit_exps(qt, h_, nmx, vred):
            K_ = (qt + 1) * 128
            nsc = qt + 1
            par = h_ % 2
            Poff = (o_P // 2) + par * T
            rsp = sb(o_rsp + par * 16, 4, F32)
            nj = ceil_div(K_, 1024)
            sbase = par * 1024 if nj == 1 else 0
            chunk_ev = {}
            for j in range(nj):
                w_j = min(1024, K_ - j * 1024)
                em("scalar", lambda sc, o=A[:, Poff + j * 1024: Poff + j * 1024 + w_j],
                   i_=PS[:, sbase + j * 1024: sbase + j * 1024 + w_j], bi=nmx,
                   a=rsp[:, j:j + 1]:
                   sc.activation(out=o, in_=i_, func=AF.Exp, bias=bi, scale=1.0,
                                 accum_out=a),
                   wait=(vred, hpar_guard[par] if j == 0 else None,
                         rsg[par] if j == 0 else None),
                   inc=s_exp)
                for sc_ in range(j * 8, min(j * 8 + 8, nsc)):
                    chunk_ev[sc_] = (s_exp, s_exp.v)
                gbank[sbase // 512 + 2 * j] = (s_exp, s_exp.v)
                gbank[sbase // 512 + 2 * j + 1] = (s_exp, s_exp.v)
            expg[par] = (s_exp, s_exp.v)
            rs = sb(o_rs, H, F32)
            if nj > 1:
                em("vector", lambda v, o=rs[:, h_:h_ + 1], i_=rsp[:, 0:nj]:
                   v.tensor_reduce(out=o, in_=i_, axis=AX.X, op=ALU.add),
                   wait=(expg[par],), inc=v_e)
            else:
                em("vector", lambda v, o=rs[:, h_:h_ + 1], i_=rsp[:, 0:1]:
                   v.tensor_copy(o, i_), wait=(expg[par],), inc=v_e)
            rsg[par] = (v_e, v_e.v)
            return chunk_ev

        def emit_PTAV(qt, h_, chunk_ev):
            nsc = qt + 1
            ngrp = ceil_div(nsc, 4)
            Poff = (o_P // 2) + (h_ % 2) * T
            avb = 6 + h_ // 8
            avo = 6 * 512 + h_ * 64
            cevs = {}

            def emit_T_C(g_):
                scs = list(range(g_ * 4, min(g_ * 4 + 4, nsc)))
                slot = g_ % 2
                bk = 4 + slot
                for idx, sc_ in enumerate(scs):
                    inb = A[:, Poff + sc_ * 128: Poff + sc_ * 128 + 128]
                    outb = bankb(bk)[:, idx * 128: idx * 128 + 128]
                    em("tensor", lambda t, o=outb, i_=inb: t.transpose(o, i_, ident),
                       wait=(chunk_ev[sc_] if (idx == 0 or sc_ == qt) else None,
                             gbank[("pt", slot)] if idx == 0 else None),
                       inc=t_pt if idx == len(scs) - 1 else None)
                tpt = (t_pt, t_pt.v)
                csem = s_e if g_ % 2 == 0 else v_e
                srcv = bankb(bk)[:, 0: len(scs) * 128]
                dstv = A[:, (o_PT // 2) + slot * 512:
                         (o_PT // 2) + slot * 512 + len(scs) * 128]
                if g_ % 2 == 0:
                    em("scalar", lambda sc, d=dstv, s=srcv: sc.copy(d, s),
                       wait=(tpt, gbank[("PT", slot)]), inc=csem)
                else:
                    em("vector", lambda v, d=dstv, s=srcv: v.tensor_copy(d, s),
                       wait=(tpt, gbank[("PT", slot)]), inc=csem)
                cevs[g_] = (csem, csem.v)
                gbank[("pt", slot)] = cevs[g_]

            def emit_AV(g_):
                scs = list(range(g_ * 4, min(g_ * 4 + 4, nsc)))
                slot = g_ % 2
                for idx, sc_ in enumerate(scs):
                    lhs = A[:, (o_PT // 2) + slot * 512 + idx * 128:
                            (o_PT // 2) + slot * 512 + idx * 128 + 128]
                    rhs = A[:, Voff + (sc_ * H + h_) * 64:
                            Voff + (sc_ * H + h_) * 64 + 64]
                    em("tensor", lambda t, o=PS[:, avo: avo + 64], l=lhs, r=rhs,
                       st_=(sc_ == 0), sp=(sc_ == nsc - 1): t.matmul(
                           o, l, r, start=st_, stop=sp),
                       wait=(cevs[g_] if idx == 0 else None,
                             gbank[avb] if sc_ == 0 else None),
                       inc=t_av if sc_ == nsc - 1 else None)
                gbank[("PT", slot)] = (t_av, t_av.v)

            for g_ in range(ngrp):
                emit_T_C(g_)
                if g_ >= 1:
                    emit_AV(g_ - 1)
            emit_AV(ngrp - 1)
            hpar_guard[h_ % 2] = (t_pt, t_pt.v)

        def emit_tail(qt):
            av_all = (t_av, t_av.v)
            rs = sb(o_rs, H, F32)
            rl = sb(o_rl, H, F32)
            em("vector", lambda v, o=rl, i_=rs: v.reciprocal(o, i_),
               wait=(rsg[0], rsg[1]), inc=v_e)
            src_av = PS[:, 6 * 512: 6 * 512 + H * 64].rearrange(
                "p (a x) -> p a x", x=64)
            dst = A[:, (o_atok // 2): (o_atok // 2) + C].rearrange(
                "p (a x) -> p a x", x=64)
            rlv = rl.unsqueeze(2).broadcast_to([128, H, 64])
            em("vector", lambda v, d=dst, s=src_av, r=rlv: v.tensor_tensor(
                out=d, in0=s, in1=r, op=ALU.mult), wait=(av_all,), inc=v_fin)
            fin = (v_fin, v_fin.v)
            gbank[6] = fin
            gbank[7] = fin
            for oc in range(NCC):
                inb = A[:, (o_atok // 2) + oc * 128: (o_atok // 2) + oc * 128 + 128]
                outb = bankb(4)[:, oc * 128: oc * 128 + 128]
                em("tensor", lambda t, o=outb, i_=inb: t.transpose(o, i_, ident),
                   wait=(fin if oc == 0 else None,
                         gbank[("pt", 0)] if oc == 0 else None),
                   inc=t_tr if oc == NCC - 1 else None)
            tat = (t_tr, t_tr.v)
            em("scalar", lambda sc, d=A[:, (o_atT // 2): (o_atT // 2) + C],
               s=bankb(4)[:, :C]: sc.copy(d, s), wait=(tat,), inc=s_e)
            atc = (s_e, s_e.v)
            gbank[("pt", 0)] = atc
            NOH = ceil_div(C, 512)
            WOH = min(512, C)
            for oh in range(NOH):
                bk = 6 + oh
                for cc in range(NCC):
                    lhs = A[:, (o_atT // 2) + cc * 128: (o_atT // 2) + cc * 128 + 128]
                    rhs = wprojA.rearrange("p (cc x) -> p cc x", x=C)[:, cc, oh * WOH: oh * WOH + WOH]
                    em("tensor", lambda t, o=bank(bk, WOH), l=lhs, r=rhs,
                       st_=(cc == 0), sp=(cc == NCC - 1): t.matmul(
                           o, l, r, start=st_, stop=sp),
                       wait=(atc if cc == 0 else None,
                             wproj_ready if cc == 0 else None,
                             gbank[bk] if cc == 0 else None),
                       inc=t_mm if cc == NCC - 1 else None)
            tproj = (t_mm, t_mm.v)
            x1buf = sb(o_x1A + (qt % 2) * C * 4, C, F32)
            em("vector", lambda v, o=x1buf, i_=PS[:, 6 * 512: 6 * 512 + C]:
               v.tensor_tensor(out=o, in0=i_, in1=bbc, op=ALU.add),
               wait=(tproj, c_bbc, gbank[("x1A", qt % 2)]), inc=None)
            em("vector", lambda v, o=x1buf, i_=x1buf,
               x_=sb(o_xA + (qt % 2) * C * 4, C, F32): v.tensor_tensor(
                   out=o, in0=i_, in1=x_, op=ALU.add),
               wait=(xload[qt],), inc=v_res)
            evac = (v_res, v_res.v)
            gbank[6] = evac
            gbank[7] = evac
            xbuf_guard[qt % 2] = evac
            if qt + 1 < NQT:
                em("scalar", lambda s, o=sb(o_xA + ((qt + 1) % 2) * C * 4, C, F32),
                   i_=x_in[b * T + (qt + 1) * 128: b * T + (qt + 2) * 128, :]:
                   s.dma_start(out=o, in_=i_),
                   wait=(xbuf_guard[(qt + 1) % 2],), inc=d_x, inc_n=16)
                xload[qt + 1] = (d_x, d_x.v)
            em("scalar", lambda s, o=x1_hbm[b * T + qt * 128: b * T + (qt + 1) * 128, :],
               i_=x1buf: s.dma_start(out=o, in_=i_),
               wait=(evac,), inc=d_st, inc_n=16)
            gbank[("x1A", qt % 2)] = (d_st, d_st.v)

        pend = None
        seq = [(qt, h_) for qt in range(NQT) for h_ in range(H)]
        emit_qpad(*seq[0])
        for idx, (qt, h_) in enumerate(seq):
            nmx, vred = emit_S_mm(qt, h_)
            if idx + 1 < len(seq):
                emit_qpad(*seq[idx + 1])
            if pend is not None:
                emit_PTAV(*pend)
                if pend[1] == H - 1:
                    emit_tail(pend[0])
            cev_ = emit_exps(qt, h_, nmx, vred)
            pend = (qt, h_, cev_)
        emit_PTAV(*pend)
        emit_tail(NQT - 1)
        if b == 0:
            return [xbuf_guard[0], xbuf_guard[1]]
        return (d_st, d_st.v)

    wq_guard = [None, None]
    qpad_guard = [None, None]
    attn_res_guard = [None, None]
    gbank[("x1A", 0)] = None
    gbank[("x1A", 1)] = None
    gbank[("PT", 0)] = None
    gbank[("PT", 1)] = None
    gbank[("pt", 0)] = None
    gbank[("pt", 1)] = None

    res_guard0 = attention(0)
    attn_res_guard[0] = res_guard0[0]
    attn_res_guard[1] = res_guard0[1]
    x1_stores_done = attention(1)

    w1sv = sb(o_w1s, ND, F32)
    w2sv = sb(o_w2s, ND, F32)
    h1sv = sb(o_h1s, ND, F32)
    h2sv = sb(o_h2s, ND, F32)
    w3sv = sb(o_w3s, NCC, F32)
    wfA = [sb(o_wfA + k * ND * 128 * 2, ND * 128) for k in range(2)]
    wfB = [sb(o_wfB + k * NCC * 128 * 2, NCC * 128) for k in range(2)]
    gbank.update({("wfA", 0): x1_stores_done, ("wfA", 1): x1_stores_done,
                  ("wfB", 0): x1_stores_done, ("wfB", 1): x1_stores_done})

    def get_unit(kind, oc, si):
        hbm, nin, slotset = {
            "w1": (w1B_hbm, NCC, "wfA"),
            "w2": (w2B_hbm, NCC, "wfB"),
            "h1": (h1B_hbm, ND, "wfA"),
            "h2": (h2B_hbm, ND, "wfA"),
            "w3": (w3B_hbm, ND, "wfA"),
        }[kind]
        slot = oc % 2
        dst = (wfA if slotset == "wfA" else wfB)[slot]
        width = nin * 128
        guard = gbank[(slotset, slot)]
        ln = d_t[tw_lane[0] % len(d_t)]
        tw_lane[0] += 1
        em("sync", lambda e, d=dst[:, :width], s=hbm[oc]: e.dma_start(
            out=d, in_=s, transpose=True),
           wait=(guard, wready[kind]), inc=ln, inc_n=16)
        return dst, (ln, ln.v)

    def ffn_sweep(si, t0, SW):
        NTC = SW // 128
        TH = ceil_div(SW, 512)
        WTH = min(512, SW)
        xbuf_guard = [gbank[("xF", 0)], gbank[("xF", 1)],
                      gbank[("x2F", 0)], gbank[("x2F", 1)]]
        bufs = [o_xF, o_xF + C * 4, o_x2F, o_x2F + C * 4]
        ln_tiles = emit_ln(x1_hbm, t0, NTC, bufs, xbuf_guard, o_xln2, SW)
        ln_done = ln_tiles[-1]

        def stage(kind_a, kind_b, src_off, dst_off, nin, sem_scale_a, sem_scale_b):
            pass

        units = {}
        units["w1", 0] = get_unit("w1", 0, si)
        units["w2", 0] = get_unit("w2", 0, si)
        for oc in range(ND):
            if oc + 1 < ND:
                units["w1", oc + 1] = get_unit("w1", oc + 1, si)
                units["w2", oc + 1] = get_unit("w2", oc + 1, si)
            ua, ra = units.pop(("w1", oc))
            ub, rb = units.pop(("w2", oc))
            sevs = []
            for part, (u, r) in (("a", (ua, ra)), ("b", (ub, rb))):
                for th in range(TH):
                    bk = (0 if part == "a" else 2) + th
                    for cc in range(NCC):
                        lhs = u[:, cc * 128:(cc + 1) * 128]
                        rhs = A[:, (o_xln2 // 2) + cc * SW + th * WTH:
                                (o_xln2 // 2) + cc * SW + th * WTH + WTH]
                        em("tensor", lambda t, o=bank(bk, WTH), l=lhs, r=rhs,
                           st_=(cc == 0), sp=(cc == NCC - 1): t.matmul(
                               o, l, r, start=st_, stop=sp),
                           wait=(r if cc == 0 else None,
                                 ln_tiles[min(th * 4 + 3, NTC - 1)] if cc == 0 else None,
                                 gbank[bk] if cc == 0 else None),
                           inc=t_mm if cc == NCC - 1 else None)
                    sevs.append(((t_mm, t_mm.v), bk, th))
            gbank[("wfA", oc % 2)] = (t_mm, t_mm.v)
            gbank[("wfB", oc % 2)] = (t_mm, t_mm.v)
            for th in range(TH):
                tma = sevs[th][0]
                bka = sevs[th][1]
                tmb = sevs[TH + th][0]
                bkb = sevs[TH + th][1]
                sl = A[:, (o_silu // 2) + th * WTH: (o_silu // 2) + th * WTH + WTH]
                em("scalar", lambda sc, o=sl, i_=bank(bka, WTH), s1=w1sv[:, oc:oc + 1]:
                   sc.activation(out=o, in_=i_, func=AF.Silu, scale=s1),
                   wait=(tma, gbank[("silu", th)]), inc=s_e)
                sev = (s_e, s_e.v)
                gbank[bka] = sev
                hdst = A[:, (o_h // 2) + oc * SW + th * WTH:
                         (o_h // 2) + oc * SW + th * WTH + WTH]
                em("vector", lambda v, o=hdst, i_=bank(bkb, WTH),
                   s1=w2sv[:, oc:oc + 1], s2=sl: v.scalar_tensor_tensor(
                       out=o, in0=i_, scalar=s1, in1=s2, op0=ALU.mult, op1=ALU.mult),
                   wait=(tmb, sev), inc=v_e)
                gbank[bkb] = (v_e, v_e.v)
                gbank[("silu", th)] = (v_e, v_e.v)
        h_done = (v_e, v_e.v)

        def gstage(kind, src_o, dst_o, scalev, actdone_prev):
            units2 = {0: get_unit(kind, 0, si)}
            out_ev = None
            for oc in range(ND):
                if oc + 1 < ND:
                    units2[oc + 1] = get_unit(kind, oc + 1, si)
                u, r = units2.pop(oc)
                tms = []
                for th in range(TH):
                    bk = (oc % 2) * 2 + th
                    for dc in range(ND):
                        lhs = u[:, dc * 128:(dc + 1) * 128]
                        rhs = A[:, (src_o // 2) + dc * SW + th * WTH:
                                (src_o // 2) + dc * SW + th * WTH + WTH]
                        em("tensor", lambda t, o=bank(bk, WTH), l=lhs, r=rhs,
                           st_=(dc == 0), sp=(dc == ND - 1): t.matmul(
                               o, l, r, start=st_, stop=sp),
                           wait=(r if dc == 0 else None,
                                 actdone_prev if dc == 0 else None,
                                 gbank[bk] if dc == 0 else None),
                           inc=t_mm if dc == ND - 1 else None)
                    tms.append(((t_mm, t_mm.v), bk))
                gbank[("wfA", oc % 2)] = (t_mm, t_mm.v)
                for th in range(TH):
                    tm, bk = tms[th]
                    dst = A[:, (dst_o // 2) + oc * SW + th * WTH:
                            (dst_o // 2) + oc * SW + th * WTH + WTH]
                    em("scalar", lambda sc, o=dst, i_=bank(bk, WTH),
                       s1=scalev[:, oc:oc + 1]: sc.activation(
                           out=o, in_=i_, func=AF.Gelu, scale=s1),
                       wait=(tm,), inc=s_e)
                    gbank[bk] = (s_e, s_e.v)
                    out_ev = (s_e, s_e.v)
            return out_ev

        g1_done = gstage("h1", o_h, o_g, h1sv, h_done)
        g2_done = gstage("h2", o_g, o_h, h2sv, g1_done)

        yreg = o_xln2
        units3 = {0: get_unit("w3", 0, si)}
        for half in range(TH):
            for oc in range(NCC):
                if (half, oc) != (TH - 1, NCC - 1):
                    nxt = oc + 1 if oc + 1 < NCC else 0
                    if nxt != 0 or half + 1 < TH:
                        if nxt not in units3:
                            units3[nxt] = get_unit("w3", nxt, si)
                u, r = units3.pop(oc) if oc in units3 else get_unit("w3", oc, si)
                bk = (oc % 2) * 2
                for dc in range(ND):
                    lhs = u[:, dc * 128:(dc + 1) * 128]
                    rhs = A[:, (o_h // 2) + dc * SW + half * WTH:
                            (o_h // 2) + dc * SW + half * WTH + WTH]
                    em("tensor", lambda t, o=bank(bk, WTH), l=lhs, r=rhs,
                       st_=(dc == 0), sp=(dc == ND - 1): t.matmul(
                           o, l, r, start=st_, stop=sp),
                       wait=(r if dc == 0 else None,
                             g2_done if dc == 0 else None,
                             gbank[bk] if dc == 0 else None),
                       inc=t_mm if dc == ND - 1 else None)
                tm = (t_mm, t_mm.v)
                gbank[("wfA", oc % 2)] = (t_mm, t_mm.v)
                ydst = sb(yreg + oc * WTH * 4, WTH, F32)
                em("scalar", lambda sc, o=ydst, i_=bank(bk, WTH),
                   s1=w3sv[:, oc:oc + 1]: sc.activation(
                       out=o, in_=i_, func=AF.Identity, scale=s1),
                   wait=(tm, gbank[("y", oc)]), inc=s_e)
                gbank[bk] = (s_e, s_e.v)
                gbank[("y", oc)] = (s_e, s_e.v)
            yev = (s_e, s_e.v)
            for tci in range(WTH // 128):
                tc = half * (WTH // 128) + tci
                xb = sb(o_xF + (tc % 2) * C * 4, C, F32)
                em("scalar", lambda s, t_=t0 + tc * 128, b_=xb: s.dma_start(
                    out=b_, in_=x1_hbm[t_: t_ + 128, :]),
                   wait=(xbuf_guard[tc % 2],), inc=d_x, inc_n=16)
                xld = (d_x, d_x.v)
                for oc in range(NCC):
                    inb = sb(yreg + oc * WTH * 4 + tci * 128 * 4, 128, F32)
                    outb = PS[:, (4 + oc // 4) * 512 + (oc % 4) * 128:
                              (4 + oc // 4) * 512 + (oc % 4) * 128 + 128]
                    em("tensor", lambda t, o=outb, i_=inb: t.transpose(o, i_, identf),
                       wait=(yev if oc == 0 else None, c_bbc if oc == 0 else None,
                             gbank[("yT",)] if oc == 0 else None),
                       inc=t_tr if oc == NCC - 1 else None)
                ttr = (t_tr, t_tr.v)
                x2b = sb(o_x2F + (tc % 2) * C * 4, C, F32)
                em("vector", lambda v, o=x2b, i_=PS[:, 4 * 512: 4 * 512 + C], x_=xb:
                   v.tensor_tensor(out=o, in0=i_, in1=x_, op=ALU.add),
                   wait=(ttr, xld, gbank[("x2F", tc % 2)]), inc=v_res)
                ev = (v_res, v_res.v)
                gbank[("yT",)] = ev
                xbuf_guard[tc % 2] = ev
                em("scalar", lambda s, t_=t0 + tc * 128, b_=x2b: s.dma_start(
                    out=out_hbm[t_: t_ + 128, :], in_=b_),
                   wait=(ev,), inc=d_st, inc_n=16)
                gbank[("x2F", tc % 2)] = (d_st, d_st.v)
            for oc in range(NCC):
                gbank[("y", oc)] = (v_res, v_res.v)
        gbank[("xF", 0)] = xbuf_guard[0]
        gbank[("xF", 1)] = xbuf_guard[1]

    gbank.update({("silu", 0): None, ("silu", 1): None,
                  ("xF", 0): (v_res, 0), ("xF", 1): None,
                  ("x2F", 0): None, ("x2F", 1): None, ("yT",): None})
    for oc in range(NCC):
        gbank[("y", oc)] = None
    gbank[("xF", 0)] = x1_stores_done
    gbank[("xF", 1)] = x1_stores_done
    gbank[("x2F", 0)] = x1_stores_done
    gbank[("x2F", 1)] = x1_stores_done

    t0 = 0
    for si, SW in enumerate(sweeps):
        ffn_sweep(si, t0, SW)
        t0 += SW

    final_st = d_st.v
    em("sync", lambda s: s.wait_ge(d_st.h, final_st))

    with nc.Block() as block:
        def run(eng_name):
            def _(eng):
                for waits, fn, inc, inc_n in prog.q[eng_name]:
                    for s, v in waits:
                        eng.wait_ge(s.h, v)
                    inst = fn(eng)
                    if inc is not None:
                        inst.then_inc(inc.h, inc_n)
            return _
        block.tensor(run("tensor"))
        block.vector(run("vector"))
        block.scalar(run("scalar"))
        block.sync(run("sync"))
        block.gpsimd(run("gpsimd"))

    return nc, st, prog


def make_inputs(cfg, full_inputs):
    B, T, C, H = cfg["B"], cfg["T"], cfg["C"], cfg["H"]
    n_cores = 16 // B if False else None
    x = full_inputs["L0_attn_x"]
    ncores = x.shape[0] // B
    ident = np.eye(128, dtype=np.float32)
    binT = np.triu(np.ones((128, 128), dtype=np.float32), k=1) * -100000.0
    bproj_bc = np.ascontiguousarray(
        np.broadcast_to(np.asarray(full_inputs["L0_attn_b_proj"], np.float32), (128, C)))
    common = dict(
        identf=ident,
        wqkv=np.ascontiguousarray(full_inputs["L0_attn_w_qkv"], np.float32),
        wproj=np.ascontiguousarray(full_inputs["L0_attn_w_proj"], np.float32),
        bproj_bc=bproj_bc,
        w1=np.ascontiguousarray(full_inputs["L0_ffn_w1_w"], np.float32),
        w2=np.ascontiguousarray(full_inputs["L0_ffn_w2_w"], np.float32),
        h1=np.ascontiguousarray(full_inputs["L0_ffn_h1_w"], np.float32),
        h2=np.ascontiguousarray(full_inputs["L0_ffn_h2_w"], np.float32),
        w3=np.ascontiguousarray(full_inputs["L0_ffn_w3_w"], np.float32),
        w1s=np.ascontiguousarray(
            np.asarray(full_inputs["L0_ffn_w1_s"], np.float32).reshape(-1, 128).T),
        w2s=np.ascontiguousarray(
            np.asarray(full_inputs["L0_ffn_w2_s"], np.float32).reshape(-1, 128).T),
        h1s=np.ascontiguousarray(
            np.asarray(full_inputs["L0_ffn_h1_s"], np.float32).reshape(-1, 128).T),
        h2s=np.ascontiguousarray(
            np.asarray(full_inputs["L0_ffn_h2_s"], np.float32).reshape(-1, 128).T),
        w3s=np.ascontiguousarray(
            np.asarray(full_inputs["L0_ffn_w3_s"], np.float32).reshape(-1, 128).T),
        ident=ident, binT=binT,
    )
    in_maps = []
    for c in range(ncores):
        m = dict(common)
        m["x"] = np.ascontiguousarray(
            np.asarray(x[c * B:(c + 1) * B], np.float32).reshape(B * T, C))
        in_maps.append(m)
    return in_maps


def kernel(L0_attn_x, L0_attn_w_qkv, L0_attn_w_proj, L0_attn_b_proj,
           L0_ffn_w1_w, L0_ffn_w1_s, L0_ffn_w2_w, L0_ffn_w2_s,
           L0_ffn_h1_w, L0_ffn_h1_s, L0_ffn_h2_w, L0_ffn_h2_s,
           L0_ffn_w3_w, L0_ffn_w3_s):
    cfg = real_cfg()
    full = dict(L0_attn_x=L0_attn_x, L0_attn_w_qkv=L0_attn_w_qkv,
                L0_attn_w_proj=L0_attn_w_proj, L0_attn_b_proj=L0_attn_b_proj,
                L0_ffn_w1_w=L0_ffn_w1_w, L0_ffn_w1_s=L0_ffn_w1_s,
                L0_ffn_w2_w=L0_ffn_w2_w, L0_ffn_w2_s=L0_ffn_w2_s,
                L0_ffn_h1_w=L0_ffn_h1_w, L0_ffn_h1_s=L0_ffn_h1_s,
                L0_ffn_h2_w=L0_ffn_h2_w, L0_ffn_h2_s=L0_ffn_h2_s,
                L0_ffn_w3_w=L0_ffn_w3_w, L0_ffn_w3_s=L0_ffn_w3_s)
    in_maps = make_inputs(cfg, full)
    nc, stack, _prog = build(cfg)
    try:
        res = run_bass_kernel_spmd(nc, in_maps, core_ids=list(range(8)))
    finally:
        stack.close()
    B, T, C = cfg["B"], cfg["T"], cfg["C"]
    outs = [r["out"].reshape(B, T, C) for r in res.results]
    return np.concatenate(outs, axis=0)


# version 23
# speedup vs baseline: 1.0792x; 1.0026x over previous
import math
from contextlib import ExitStack

import numpy as np
import ml_dtypes

import concourse.bass as bass
import concourse.mybir as mybir
from concourse.bass_utils import run_bass_kernel_spmd

F32 = mybir.dt.float32
BF16 = mybir.dt.bfloat16
AF = mybir.ActivationFunctionType
ALU = mybir.AluOpType
AX = mybir.AxisListType

BIG = 3.0e38
ENGINES = ("tensor", "vector", "scalar", "sync", "gpsimd")


def real_cfg():
    return dict(B=2, T=2048, C=1024, H=16, HS=64, DH=4096,
                sweeps=(1024, 1024, 1024, 1024))


class Sem:
    __slots__ = ("h", "v", "name")

    def __init__(self, h, name):
        self.h = h
        self.v = 0
        self.name = name


class Prog:
    def __init__(self):
        self.q = {e: [] for e in ENGINES}
        self.waited = {e: {} for e in ENGINES}
        self.self_sem = {}

    def emit(self, eng, fn, wait=(), inc=None, inc_n=1):
        wait = list(wait)
        if inc is not None and inc_n >= 16 and inc.v > 0:
            wait.append((inc, inc.v))
        ss = self.self_sem.get(eng)
        if ss is not None and not (inc is not None and inc_n >= 16):
            assert inc is None or inc is ss, f"{eng} must inc its engine sem"
            if ss.v > 0:
                wait.append((ss, ss.v))
            inc, inc_n = ss, 1
        w = []
        for item in wait:
            if item is None:
                continue
            s, v = item
            if v <= 0:
                continue
            seen = self.waited[eng].get(id(s), 0)
            if v > seen:
                self.waited[eng][id(s)] = v
                w.append((s, v))
        tgt = None
        if inc is not None:
            inc.v += inc_n
            tgt = inc.v
        self.q[eng].append((tuple(w), fn, inc, inc_n))
        return (inc, tgt) if inc is not None else None


def ceil_div(a, b):
    return (a + b - 1) // b


def build(cfg):
    B, T, C, H, HS, DH = (cfg[k] for k in ("B", "T", "C", "H", "HS", "DH"))
    sweeps = cfg["sweeps"]
    assert HS == 64 and H % 2 == 0 and H * HS == C
    assert C % 128 == 0 and T % 128 == 0 and DH % 128 == 0
    assert sum(sweeps) == B * T and all(s % 128 == 0 for s in sweeps)
    NCC = C // 128
    NQT = T // 128
    NHP = H // 2
    TOK = B * T
    ND = DH // 128
    SWMAX = max(sweeps)
    NBAV = ceil_div(H, 7)
    AVB = 8 - NBAV
    nstat = ceil_div(C, 512)

    nc = bass.Bass("TRN2", target_bir_lowering=False)

    dt = nc.dram_tensor
    x_in = dt("x", [TOK, C], F32, kind="ExternalInput").ap()
    wqkv = dt("wqkv", [H, 3 * HS, C], F32, kind="ExternalInput").ap()
    wproj = dt("wproj", [C, C], F32, kind="ExternalInput").ap()
    bproj_bc = dt("bproj_bc", [128, C], F32, kind="ExternalInput").ap()
    w1 = dt("w1", [DH, C], F32, kind="ExternalInput").ap()
    w2 = dt("w2", [DH, C], F32, kind="ExternalInput").ap()
    h1 = dt("h1", [DH, DH], F32, kind="ExternalInput").ap()
    h2 = dt("h2", [DH, DH], F32, kind="ExternalInput").ap()
    w3 = dt("w3", [C, DH], F32, kind="ExternalInput").ap()
    w1s = dt("w1s", [128, ND], F32, kind="ExternalInput").ap()
    w2s = dt("w2s", [128, ND], F32, kind="ExternalInput").ap()
    h1s = dt("h1s", [128, ND], F32, kind="ExternalInput").ap()
    h2s = dt("h2s", [128, ND], F32, kind="ExternalInput").ap()
    w3s = dt("w3s", [128, NCC], F32, kind="ExternalInput").ap()
    ident_in = dt("ident", [128, 128], F32, kind="ExternalInput").ap()
    binT_in = dt("binT", [128, 128], F32, kind="ExternalInput").ap()
    identf_in = dt("identf", [128, 128], F32, kind="ExternalInput").ap()
    out_hbm = dt("out", [TOK, C], F32, kind="ExternalOutput").ap()

    x1_hbm = dt("x1_scratch", [TOK, C], F32, kind="Internal").ap()
    qkvB_hbm = dt("qkvB", [NHP, NCC * 384, 128], BF16, kind="Internal").ap()
    projB_hbm = dt("projB", [NCC, NCC * 128, 128], BF16, kind="Internal").ap()
    w1B_hbm = dt("w1B", [ND, NCC * 128, 128], BF16, kind="Internal").ap()
    w2B_hbm = dt("w2B", [ND, NCC * 128, 128], BF16, kind="Internal").ap()
    h1B_hbm = dt("h1B", [ND, ND * 128, 128], BF16, kind="Internal").ap()
    h2B_hbm = dt("h2B", [ND, ND * 128, 128], BF16, kind="Internal").ap()
    w3B_hbm = dt("w3B", [NCC, ND * 128, 128], BF16, kind="Internal").ap()

    st = ExitStack()
    class Alloc:
        def __init__(self):
            self.off = 0

        def take(self, nbytes, align=4):
            self.off = (self.off + align - 1) // align * align
            o = self.off
            self.off += nbytes
            return o

    al = Alloc()
    o_ident = al.take(128 * 2)
    o_binT = al.take(128 * 2)
    o_identf = al.take(128 * 4)
    o_bbc = al.take(C * 4)
    o_w1s = al.take(ND * 4)
    o_w2s = al.take(ND * 4)
    o_h1s = al.take(ND * 4)
    o_h2s = al.take(ND * 4)
    o_w3s = al.take(NCC * 4)
    o_stats = al.take(4 * 12 * 4)
    o_mv = al.take(4 * 2 * 4)
    o_std = al.take(4 * 4)
    o_istd = al.take(4 * 4)
    o_lntok = al.take(2 * C * 2)
    o_nmp = al.take(2 * 4 * 4)
    o_rsp = al.take(2 * 4 * 4)
    o_rs = al.take(H * 4)
    o_rl = al.take(H * 4)
    p0 = al.off
    al.off = p0
    o_xlnA = al.take(NCC * T * 2)
    o_Q = al.take(NHP * T * 2)
    o_K = al.take(NHP * T * 2)
    o_V = al.take((T // 128) * H * 64 * 2)
    o_P = al.take(2 * T * 2)
    o_PT = al.take(2 * 512 * 2)
    o_qpad = al.take(2 * 128 * 2)
    o_atok = al.take(C * 2)
    o_atT = al.take(C * 2)
    o_wqkvA = al.take(2 * NCC * 384 * 2, align=32)
    o_wprojA = al.take(NCC * C * 2, align=32)
    o_xA = al.take(2 * C * 4)
    o_x1A = al.take(2 * C * 4)
    endA = al.off
    al.off = p0
    o_xln2 = al.take(NCC * SWMAX * 2)
    o_h = al.take(ND * SWMAX * 2)
    o_g = al.take(ND * SWMAX * 2)
    o_wfA = al.take(2 * ND * 128 * 2, align=32)
    o_wfB = al.take(2 * NCC * 128 * 2, align=32)
    o_silu = al.take(SWMAX * 2)
    o_xF = al.take(2 * C * 4)
    o_x2F = al.take(2 * C * 4)
    endF = al.off
    total = max(endA, endF)
    arena_elems = ceil_div(total, 2) + 64

    arena = st.enter_context(nc.sbuf_tensor("arena", [128, arena_elems], BF16))
    ps_t = st.enter_context(nc.psum_tensor("ps", [128, 4096], F32))
    A = arena.ap()
    PS = ps_t.ap()

    def sb(off, n, dtype=BF16, parts=(0, 128)):
        if dtype == BF16:
            v = A[parts[0]:parts[1], off // 2: off // 2 + n]
        else:
            v = A[parts[0]:parts[1], off // 2: off // 2 + 2 * n].bitcast(F32)
        return v

    def bank(i, w=512):
        return PS[:, i * 512: i * 512 + w]

    def bankb(i, w=1024):
        return PS[:, i * 512: (i + 1) * 512].bitcast(BF16)[:, :w]

    ident = sb(o_ident, 128)
    mtri = sb(o_binT, 128)
    identf = sb(o_identf, 128, F32)
    bbc = sb(o_bbc, C, F32)

    prog = Prog()
    sems = []

    def mksem(name):
        s = Sem(st.enter_context(nc.semaphore(name)), name)
        sems.append(s)
        return s

    d_x = mksem("d_x")
    d_x2 = mksem("d_x2")
    d_st = mksem("d_st")
    d_w = mksem("d_w")
    d_t = [mksem(f"d_t{i}") for i in range(6)]
    t_mm = mksem("t_mm")
    t_s = mksem("t_s")
    t_pt = mksem("t_pt")
    t_av = mksem("t_av")
    t_tr = mksem("t_tr")
    v_e = mksem("v")
    v_red = v_fin = v_res = v_e
    s_e = mksem("s")
    s_exp = s_wt = s_e
    prog.self_sem = {"vector": v_e, "scalar": s_e}

    gbank = {i: None for i in range(8)}
    tw_lane = [0, 0]

    def em(eng, fn, wait=(), inc=None, inc_n=1):
        return prog.emit(eng, fn, wait, inc, inc_n)

    em("gpsimd", lambda g: g.dma_start(out=ident, in_=ident_in), inc=d_w, inc_n=16)
    em("gpsimd", lambda g: g.dma_start(out=mtri, in_=binT_in), inc=d_w, inc_n=16)
    em("scalar", lambda s: s.dma_start(out=identf, in_=identf_in), inc=d_x, inc_n=16)
    em("scalar", lambda s: s.dma_start(out=bbc, in_=bproj_bc), inc=d_x, inc_n=16)
    for name, off, src, n in (("w1s", o_w1s, w1s, ND), ("w2s", o_w2s, w2s, ND),
                              ("h1s", o_h1s, h1s, ND), ("h2s", o_h2s, h2s, ND),
                              ("w3s", o_w3s, w3s, NCC)):
        dstv = sb(off, n, F32)
        em("scalar", lambda s, d=dstv, sv=src: s.dma_start(out=d, in_=sv),
           inc=d_x, inc_n=16)
    c_after_const = (d_w, d_w.v)
    c_bbc = (d_x, d_x.v)

    wqkvA = [sb(o_wqkvA + k * NCC * 384 * 2, NCC * 384) for k in range(2)]
    wprojA = sb(o_wprojA, NCC * C)

    qkv_cast = [None] * NHP
    for hp in range(NHP):
        for half in range(2):
            s_ap = wqkv[2 * hp + half].rearrange(
                "(part r) (cc c) -> cc part r c", r=HS, c=128)
            d_ap = qkvB_hbm[hp].rearrange(
                "(cc part half r) c -> cc part half r c", part=3, half=2, r=HS)[
                :, :, half]
            em("gpsimd", lambda g, d=d_ap, s_=s_ap: g.dma_start(out=d, in_=s_),
               inc=d_w, inc_n=16)
        qkv_cast[hp] = (d_w, d_w.v)
    for oc in range(NCC):
        s_ap = wproj[oc * 128:(oc + 1) * 128, :].rearrange(
            "r (cc c) -> cc r c", c=128)
        d_ap = projB_hbm[:, oc * 128:(oc + 1) * 128, :]
        em("gpsimd", lambda g, d=d_ap, s_=s_ap: g.dma_start(out=d, in_=s_),
           inc=d_w, inc_n=16)
    proj_cast = (d_w, d_w.v)
    em("sync", lambda e: e.dma_start(
        out=wprojA, in_=projB_hbm.rearrange("a b c -> (a b) c"), transpose=True),
       wait=(proj_cast,), inc=d_t[0], inc_n=16)
    wproj_ready = (d_t[0], d_t[0].v)

    wready = {}
    for kind, src, dst, nin, nout in (("w1", w1, w1B_hbm, NCC, ND),
                                      ("w2", w2, w2B_hbm, NCC, ND),
                                      ("h1", h1, h1B_hbm, ND, ND),
                                      ("h2", h2, h2B_hbm, ND, ND),
                                      ("w3", w3, w3B_hbm, ND, NCC)):
        for cc in range(nin):
            s_ap = src[:, cc * 128:(cc + 1) * 128].rearrange(
                "(oc r) c -> oc r c", r=128)
            d_ap = dst[:, cc * 128:(cc + 1) * 128, :]
            em("gpsimd", lambda g, d=d_ap, s_=s_ap: g.dma_start(out=d, in_=s_),
               inc=d_w, inc_n=16)
        wready[kind] = (d_w, d_w.v)

    stats4 = [sb(o_stats + k * 12 * 4, 12, F32) for k in range(4)]
    mvs4 = [sb(o_mv + k * 2 * 4, 2, F32) for k in range(4)]
    vars4 = sb(o_mv, 8, F32).rearrange("p (a b) -> p a b", b=2)[:, :, 1:2]
    stdv = sb(o_std, 4, F32)
    istdv = sb(o_istd, 4, F32)

    def emit_ln(src_hbm, row0, ntiles, bufs, xbuf_guard, dst_off, dst_T,
                tbanks=(4, 5)):
        nb = len(bufs)
        lanes_x = (d_x, d_x2)

        def load(i):
            buf = sb(bufs[i % nb], C, F32)
            ln_ = lanes_x[i % 2]
            em("scalar", lambda s, b=buf: s.dma_start(
                out=b, in_=src_hbm[row0 + i * 128: row0 + (i + 1) * 128, :]),
               wait=(xbuf_guard[i % nb],), inc=ln_, inc_n=16)
            return (ln_, ln_.v)

        tile_done = []
        last_ttr = [None, None]
        for g in range(ceil_div(ntiles, 4)):
            tiles = list(range(g * 4, min(g * 4 + 4, ntiles)))
            gn = len(tiles)
            sevs = {}
            for i in tiles:
                ld = load(i)
                buf = sb(bufs[i % nb], C, F32)
                for k in range(nstat):
                    w = min(512, C - k * 512)
                    em("vector", lambda v, st_=stats4[i % 4][:, k * 6:(k + 1) * 6],
                       b=buf[:, k * 512: k * 512 + w]: v.bn_stats(st_, b),
                       wait=(ld,), inc=v_e)
                sevs[i] = (v_e, v_e.v)
            for i in tiles:
                em("vector", lambda v, m=mvs4[i % 4],
                   st_=stats4[i % 4][:, :6 * nstat]: v.bn_aggr(m, st_),
                   wait=(sevs[i],), inc=v_e)
            aggr_ev = (v_e, v_e.v)
            em("vector", lambda v: v.tensor_scalar(
                out=vars4[:, 0:gn], in0=vars4[:, 0:gn], scalar1=1e-5,
                scalar2=None, op0=ALU.add), wait=(aggr_ev,), inc=v_e)
            ve = (v_e, v_e.v)
            em("scalar", lambda sc, o=stdv[:, 0:gn], i_=vars4[:, 0:gn]:
               sc.activation(out=o, in_=i_, func=AF.Sqrt), wait=(ve,), inc=s_e)
            se = (s_e, s_e.v)
            em("vector", lambda v, o=istdv[:, 0:gn], i_=stdv[:, 0:gn]:
               v.reciprocal(o, i_), wait=(se,), inc=v_e)
            for i in tiles:
                buf = sb(bufs[i % nb], C, F32)
                xlntok = sb(o_lntok + (i % 2) * C * 2, C)
                em("vector", lambda v, o=xlntok, b=buf, m=mvs4[i % 4],
                   i_=istdv[:, (i % 4):(i % 4) + 1]:
                   v.tensor_scalar(out=o, in0=b, scalar1=m[:, 0:1], scalar2=i_,
                                   op0=ALU.subtract, op1=ALU.mult),
                   wait=(gbank[tbanks[i % 2]], last_ttr[i % 2]), inc=v_e)
                xbuf_guard[i % nb] = (v_e, v_e.v)
                applied = (v_e, v_e.v)
                bk = tbanks[i % 2]
                for cc in range(NCC):
                    inb = xlntok[:, cc * 128:(cc + 1) * 128]
                    outb = bankb(bk)[:, cc * 128:(cc + 1) * 128]
                    em("tensor", lambda t, o=outb, i_=inb: t.transpose(o, i_, ident),
                       wait=(applied if cc == 0 else None,
                             c_after_const if cc == 0 else None,
                             gbank[bk] if cc == 0 else None),
                       inc=t_tr if cc == NCC - 1 else None)
                ttr = (t_tr, t_tr.v)
                last_ttr[i % 2] = ttr
                dstv = A[:, dst_off // 2: dst_off // 2 + NCC * dst_T].rearrange(
                    "p (cc x) -> p cc x", x=dst_T)[:, :, i * 128:(i + 1) * 128]
                srcv = bankb(bk)[:, :NCC * 128].rearrange("p (cc x) -> p cc x", x=128)
                em("scalar", lambda sc, d=dstv, s=srcv: sc.copy(d, s),
                   wait=(ttr,), inc=s_e)
                gbank[bk] = (s_e, s_e.v)
                tile_done.append((s_e, s_e.v))
        return tile_done

    xlnA_ready = None

    def attention(b):
        nonlocal xlnA_ready
        if b == 0:
            xbuf_guard = [None, None, None, None]
        else:
            xbuf_guard = [attn_res_guard[0], attn_res_guard[1],
                          gbank[("x1A", 0)], gbank[("x1A", 1)]]
        bufs = [o_xA, o_xA + C * 4, o_x1A, o_x1A + C * 4]
        ln_tiles = emit_ln(x_in, b * T, NQT, bufs, xbuf_guard, o_xlnA, T)
        ln_done = ln_tiles[-1]
        xlnA_ready = ln_done

        vavprev = (t_av, t_av.v)
        Voff = o_V // 2
        NT5 = T // 512 if T >= 512 else 1
        W5 = min(512, T)
        for hp in range(NHP):
            slot = hp % 2
            unit = wqkvA[slot]
            ln_q = d_t[tw_lane[0] % len(d_t)]
            tw_lane[0] += 1
            em("sync", lambda e, u=unit, hp_=hp: e.dma_start(
                out=u, in_=qkvB_hbm[hp_], transpose=True),
               wait=(qkv_cast[hp], wq_guard[slot]), inc=ln_q, inc_n=16)
            unit_rdy = (ln_q, ln_q.v)
            uv = unit.rearrange("p (cc x) -> p cc x", x=384)
            for part, dsto, evac_eng in ((0, o_Q, "scalar"), (1, o_K, "vector")):
                for tt in range(NT5):
                    bk = part * 2 + (tt % 2) if False else (0 if part == 0 else 1)
                    bk = (tt % 2) + (0 if part == 0 else 2)
                    for cc in range(NCC):
                        lhs = uv[:, cc, part * 128:(part + 1) * 128]
                        rhs = A[:, (o_xlnA // 2) + cc * T + tt * W5:
                                (o_xlnA // 2) + cc * T + tt * W5 + W5]
                        em("tensor", lambda t, o=bank(bk, W5), l=lhs, r=rhs,
                           st_=(cc == 0), sp=(cc == NCC - 1): t.matmul(
                               o, l, r, start=st_, stop=sp),
                           wait=(unit_rdy if cc == 0 else None,
                                 xlnA_ready if cc == 0 else None,
                                 gbank[bk] if cc == 0 else None),
                           inc=t_mm if cc == NCC - 1 else None)
                    tm = (t_mm, t_mm.v)
                    dst = A[:, (dsto // 2) + hp * T + tt * W5:
                            (dsto // 2) + hp * T + tt * W5 + W5]
                    if part == 0:
                        em("scalar", lambda sc, d=dst, s=bank(bk, W5):
                           sc.mul(d, s, 0.125), wait=(tm,), inc=s_e)
                        gbank[bk] = (s_e, s_e.v)
                    else:
                        em("vector", lambda v, d=dst, s=bank(bk, W5):
                           v.tensor_copy(d, s), wait=(tm,), inc=v_e)
                        gbank[bk] = (v_e, v_e.v)
            for sc_ in range(T // 128):
                bk = 4 + (sc_ % 2)
                for cc in range(NCC):
                    lhs = A[:, (o_xlnA // 2) + cc * T + sc_ * 128:
                            (o_xlnA // 2) + cc * T + (sc_) * 128 + 128]
                    rhs = uv[:, cc, 256:384]
                    em("tensor", lambda t, o=bank(bk, 128), l=lhs, r=rhs,
                       st_=(cc == 0), sp=(cc == NCC - 1): t.matmul(
                           o, l, r, start=st_, stop=sp),
                       wait=(unit_rdy if cc == 0 else None,
                             xlnA_ready if cc == 0 else None,
                             gbank[bk] if cc == 0 else None),
                       inc=t_mm if cc == NCC - 1 else None)
                tm = (t_mm, t_mm.v)
                dst = A[:, Voff + (sc_ * H + 2 * hp) * 64:
                        Voff + (sc_ * H + 2 * hp) * 64 + 128]
                em("vector", lambda v, d=dst, s=bank(bk, 128): v.tensor_copy(d, s),
                   wait=(tm, vavprev), inc=v_e)
                gbank[bk] = (v_e, v_e.v)
            wq_guard[slot] = (t_mm, t_mm.v)
        qkv_done_t = (t_mm, t_mm.v)
        qkv_done_v = (v_e, v_e.v)
        qkv_done_s = (s_e, s_e.v)

        em("scalar", lambda s: s.dma_start(
            out=sb(o_xA, C, F32), in_=x_in[b * T: b * T + 128, :]),
           wait=(xbuf_guard[0],), inc=d_x, inc_n=16)
        xload = {0: (d_x, d_x.v)}
        hpar_guard = [None, None]
        expg = [None, None]
        rsg = [None, None]
        if b == 0:
            em("vector", lambda v: v.memset(
                A[:, o_qpad // 2: o_qpad // 2 + 256], 0.0), inc=v_e)
            qpad_init = (v_e, v_e.v)
            qpad_guard[0] = qpad_guard[1] = qpad_init
        qpad_evs = [None, None]

        def emit_qpad(qt, h_):
            par = h_ % 2
            lo = par * 64
            hpv = (h_ // 2) * T
            qp = (o_qpad // 2) + par * 128
            em("scalar", lambda sc,
               d=A[lo:lo + 64, qp: qp + 128],
               s=A[lo:lo + 64, (o_Q // 2) + hpv + qt * 128:
                   (o_Q // 2) + hpv + qt * 128 + 128]: sc.copy(d, s),
               wait=(qpad_guard[par],), inc=s_e)
            qpad_evs[par] = (s_e, s_e.v)

        def emit_S_mm(qt, h_):
            K_ = (qt + 1) * 128
            npiece = ceil_div(K_, 1024)
            qtw = qt * 128
            pd = qtw // 1024
            hp, lo = h_ // 2, (h_ % 2) * 64
            hpv = hp * T
            par = h_ % 2
            nm = sb(o_nmp + par * 16, 4, F32)
            qp = (o_qpad // 2) + par * 128
            qpad_ev = qpad_evs[par]
            sbase = par * 1024 if npiece == 1 else 0
            ts_p = []
            for p in range(npiece):
                w_p = min(1024, K_ - p * 1024)
                for jj in range(ceil_div(w_p, 512)):
                    w_j = min(512, w_p - jj * 512)
                    bk = sbase // 512 + 2 * p + jj
                    off = p * 1024 + jj * 512
                    lhs = A[:, qp: qp + 128]
                    rhs = A[:, (o_K // 2) + hpv + off:
                            (o_K // 2) + hpv + off + w_j]
                    em("tensor", lambda t, o=bank(bk, w_j), l=lhs, r=rhs:
                       t.matmul(o, l, r, start=True, stop=True),
                       wait=(gbank[bk], qpad_ev,
                             qkv_done_v if qt == 0 and h_ == 0 else None,
                             qkv_done_s if qt == 0 and h_ == 0 else None),
                       inc=t_s if jj == ceil_div(w_p, 512) - 1 else None)
                ts_p.append((t_s, t_s.v))
                if p < pd:
                    em("vector", lambda v, o=nm[:, p:p + 1],
                       i_=PS[:, sbase + p * 1024: sbase + p * 1024 + 1024]:
                       v.tensor_reduce(out=o, in_=i_, axis=AX.X, op=ALU.max,
                                       negate=True),
                       wait=(ts_p[p], expg[h_ % 2]))
            qpad_guard[par] = (t_s, t_s.v)
            em("vector", lambda v, o=PS[:, sbase + qtw: sbase + qtw + 128], m=mtri:
               v.tensor_tensor(out=o, in0=o, in1=m, op=ALU.add),
               wait=(ts_p[pd], expg[h_ % 2]))
            w_pd = K_ - pd * 1024
            em("vector", lambda v, o=nm[:, pd:pd + 1],
               i_=PS[:, sbase + pd * 1024: sbase + pd * 1024 + w_pd]:
               v.tensor_reduce(out=o, in_=i_, axis=AX.X, op=ALU.max,
                               negate=True))
            if npiece > 1:
                em("vector", lambda v, o=nm[:, 3:4], i_=nm[:, 0:npiece]:
                   v.tensor_reduce(out=o, in_=i_, axis=AX.X, op=ALU.min,
                                   negate=False))
            nmx = nm[:, 3:4] if npiece > 1 else nm[:, 0:1]
            return nmx, (v_e, v_e.v)

        def emit_exps(qt, h_, nmx, vred):
            K_ = (qt + 1) * 128
            nsc = qt + 1
            par = h_ % 2
            Poff = (o_P // 2) + par * T
            rsp = sb(o_rsp + par * 16, 4, F32)
            nj = ceil_div(K_, 1024)
            sbase = par * 1024 if nj == 1 else 0
            chunk_ev = {}
            for j in range(nj):
                w_j = min(1024, K_ - j * 1024)
                em("scalar", lambda sc, o=A[:, Poff + j * 1024: Poff + j * 1024 + w_j],
                   i_=PS[:, sbase + j * 1024: sbase + j * 1024 + w_j], bi=nmx,
                   a=rsp[:, j:j + 1]:
                   sc.activation(out=o, in_=i_, func=AF.Exp, bias=bi, scale=1.0,
                                 accum_out=a),
                   wait=(vred, hpar_guard[par] if j == 0 else None,
                         rsg[par] if j == 0 else None),
                   inc=s_exp)
                for sc_ in range(j * 8, min(j * 8 + 8, nsc)):
                    chunk_ev[sc_] = (s_exp, s_exp.v)
                gbank[sbase // 512 + 2 * j] = (s_exp, s_exp.v)
                gbank[sbase // 512 + 2 * j + 1] = (s_exp, s_exp.v)
            expg[par] = (s_exp, s_exp.v)
            rs = sb(o_rs, H, F32)
            if nj > 1:
                em("vector", lambda v, o=rs[:, h_:h_ + 1], i_=rsp[:, 0:nj]:
                   v.tensor_reduce(out=o, in_=i_, axis=AX.X, op=ALU.add),
                   wait=(expg[par],), inc=v_e)
            else:
                em("vector", lambda v, o=rs[:, h_:h_ + 1], i_=rsp[:, 0:1]:
                   v.tensor_copy(o, i_), wait=(expg[par],), inc=v_e)
            rsg[par] = (v_e, v_e.v)
            return chunk_ev

        def emit_PTAV(qt, h_, chunk_ev):
            nsc = qt + 1
            ngrp = ceil_div(nsc, 4)
            Poff = (o_P // 2) + (h_ % 2) * T
            avb = 6 + h_ // 8
            avo = 6 * 512 + h_ * 64
            cevs = {}

            def emit_T_C(g_):
                scs = list(range(g_ * 4, min(g_ * 4 + 4, nsc)))
                slot = g_ % 2
                bk = 4 + slot
                for idx, sc_ in enumerate(scs):
                    inb = A[:, Poff + sc_ * 128: Poff + sc_ * 128 + 128]
                    outb = bankb(bk)[:, idx * 128: idx * 128 + 128]
                    em("tensor", lambda t, o=outb, i_=inb: t.transpose(o, i_, ident),
                       wait=(chunk_ev[sc_] if (idx == 0 or sc_ == qt) else None,
                             gbank[("pt", slot)] if idx == 0 else None),
                       inc=t_pt if idx == len(scs) - 1 else None)
                tpt = (t_pt, t_pt.v)
                csem = s_e if g_ % 2 == 0 else v_e
                srcv = bankb(bk)[:, 0: len(scs) * 128]
                dstv = A[:, (o_PT // 2) + slot * 512:
                         (o_PT // 2) + slot * 512 + len(scs) * 128]
                if g_ % 2 == 0:
                    em("scalar", lambda sc, d=dstv, s=srcv: sc.copy(d, s),
                       wait=(tpt, gbank[("PT", slot)]), inc=csem)
                else:
                    em("vector", lambda v, d=dstv, s=srcv: v.tensor_copy(d, s),
                       wait=(tpt, gbank[("PT", slot)]), inc=csem)
                cevs[g_] = (csem, csem.v)
                gbank[("pt", slot)] = cevs[g_]

            def emit_AV(g_):
                scs = list(range(g_ * 4, min(g_ * 4 + 4, nsc)))
                slot = g_ % 2
                for idx, sc_ in enumerate(scs):
                    lhs = A[:, (o_PT // 2) + slot * 512 + idx * 128:
                            (o_PT // 2) + slot * 512 + idx * 128 + 128]
                    rhs = A[:, Voff + (sc_ * H + h_) * 64:
                            Voff + (sc_ * H + h_) * 64 + 64]
                    em("tensor", lambda t, o=PS[:, avo: avo + 64], l=lhs, r=rhs,
                       st_=(sc_ == 0), sp=(sc_ == nsc - 1): t.matmul(
                           o, l, r, start=st_, stop=sp),
                       wait=(cevs[g_] if idx == 0 else None,
                             gbank[avb] if sc_ == 0 else None),
                       inc=t_av if sc_ == nsc - 1 else None)
                gbank[("PT", slot)] = (t_av, t_av.v)

            for g_ in range(ngrp):
                emit_T_C(g_)
                if g_ >= 1:
                    emit_AV(g_ - 1)
            emit_AV(ngrp - 1)
            hpar_guard[h_ % 2] = (t_pt, t_pt.v)

        def emit_tail(qt):
            av_all = (t_av, t_av.v)
            rs = sb(o_rs, H, F32)
            rl = sb(o_rl, H, F32)
            em("vector", lambda v, o=rl, i_=rs: v.reciprocal(o, i_),
               wait=(rsg[0], rsg[1]), inc=v_e)
            src_av = PS[:, 6 * 512: 6 * 512 + H * 64].rearrange(
                "p (a x) -> p a x", x=64)
            dst = A[:, (o_atok // 2): (o_atok // 2) + C].rearrange(
                "p (a x) -> p a x", x=64)
            rlv = rl.unsqueeze(2).broadcast_to([128, H, 64])
            em("vector", lambda v, d=dst, s=src_av, r=rlv: v.tensor_tensor(
                out=d, in0=s, in1=r, op=ALU.mult), wait=(av_all,), inc=v_fin)
            fin = (v_fin, v_fin.v)
            gbank[6] = fin
            gbank[7] = fin
            for oc in range(NCC):
                inb = A[:, (o_atok // 2) + oc * 128: (o_atok // 2) + oc * 128 + 128]
                outb = bankb(4)[:, oc * 128: oc * 128 + 128]
                em("tensor", lambda t, o=outb, i_=inb: t.transpose(o, i_, ident),
                   wait=(fin if oc == 0 else None,
                         gbank[("pt", 0)] if oc == 0 else None),
                   inc=t_tr if oc == NCC - 1 else None)
            tat = (t_tr, t_tr.v)
            em("scalar", lambda sc, d=A[:, (o_atT // 2): (o_atT // 2) + C],
               s=bankb(4)[:, :C]: sc.copy(d, s), wait=(tat,), inc=s_e)
            atc = (s_e, s_e.v)
            gbank[("pt", 0)] = atc
            NOH = ceil_div(C, 512)
            WOH = min(512, C)
            for oh in range(NOH):
                bk = 6 + oh
                for cc in range(NCC):
                    lhs = A[:, (o_atT // 2) + cc * 128: (o_atT // 2) + cc * 128 + 128]
                    rhs = wprojA.rearrange("p (cc x) -> p cc x", x=C)[:, cc, oh * WOH: oh * WOH + WOH]
                    em("tensor", lambda t, o=bank(bk, WOH), l=lhs, r=rhs,
                       st_=(cc == 0), sp=(cc == NCC - 1): t.matmul(
                           o, l, r, start=st_, stop=sp),
                       wait=(atc if cc == 0 else None,
                             wproj_ready if cc == 0 else None,
                             gbank[bk] if cc == 0 else None),
                       inc=t_mm if cc == NCC - 1 else None)
            tproj = (t_mm, t_mm.v)
            x1buf = sb(o_x1A + (qt % 2) * C * 4, C, F32)
            em("vector", lambda v, o=x1buf, i_=PS[:, 6 * 512: 6 * 512 + C]:
               v.tensor_tensor(out=o, in0=i_, in1=bbc, op=ALU.add),
               wait=(tproj, c_bbc, gbank[("x1A", qt % 2)]), inc=None)
            em("vector", lambda v, o=x1buf, i_=x1buf,
               x_=sb(o_xA + (qt % 2) * C * 4, C, F32): v.tensor_tensor(
                   out=o, in0=i_, in1=x_, op=ALU.add),
               wait=(xload[qt],), inc=v_res)
            evac = (v_res, v_res.v)
            gbank[6] = evac
            gbank[7] = evac
            xbuf_guard[qt % 2] = evac
            if qt + 1 < NQT:
                em("scalar", lambda s, o=sb(o_xA + ((qt + 1) % 2) * C * 4, C, F32),
                   i_=x_in[b * T + (qt + 1) * 128: b * T + (qt + 2) * 128, :]:
                   s.dma_start(out=o, in_=i_),
                   wait=(xbuf_guard[(qt + 1) % 2],), inc=d_x, inc_n=16)
                xload[qt + 1] = (d_x, d_x.v)
            em("scalar", lambda s, o=x1_hbm[b * T + qt * 128: b * T + (qt + 1) * 128, :],
               i_=x1buf: s.dma_start(out=o, in_=i_),
               wait=(evac,), inc=d_st, inc_n=16)
            gbank[("x1A", qt % 2)] = (d_st, d_st.v)

        pend = None
        seq = [(qt, h_) for qt in range(NQT) for h_ in range(H)]
        emit_qpad(*seq[0])
        for idx, (qt, h_) in enumerate(seq):
            nmx, vred = emit_S_mm(qt, h_)
            if idx + 1 < len(seq):
                emit_qpad(*seq[idx + 1])
            if pend is not None:
                emit_PTAV(*pend)
                if pend[1] == H - 1:
                    emit_tail(pend[0])
            cev_ = emit_exps(qt, h_, nmx, vred)
            pend = (qt, h_, cev_)
        emit_PTAV(*pend)
        emit_tail(NQT - 1)
        if b == 0:
            return [xbuf_guard[0], xbuf_guard[1]]
        return (d_st, d_st.v)

    wq_guard = [None, None]
    qpad_guard = [None, None]
    attn_res_guard = [None, None]
    gbank[("x1A", 0)] = None
    gbank[("x1A", 1)] = None
    gbank[("PT", 0)] = None
    gbank[("PT", 1)] = None
    gbank[("pt", 0)] = None
    gbank[("pt", 1)] = None

    res_guard0 = attention(0)
    attn_res_guard[0] = res_guard0[0]
    attn_res_guard[1] = res_guard0[1]
    x1_stores_done = attention(1)

    w1sv = sb(o_w1s, ND, F32)
    w2sv = sb(o_w2s, ND, F32)
    h1sv = sb(o_h1s, ND, F32)
    h2sv = sb(o_h2s, ND, F32)
    w3sv = sb(o_w3s, NCC, F32)
    wfA = [sb(o_wfA + k * ND * 128 * 2, ND * 128) for k in range(2)]
    wfB = [sb(o_wfB + k * NCC * 128 * 2, NCC * 128) for k in range(2)]
    gbank.update({("wfA", 0): x1_stores_done, ("wfA", 1): x1_stores_done,
                  ("wfB", 0): x1_stores_done, ("wfB", 1): x1_stores_done})

    def get_unit(kind, oc, si):
        hbm, nin, slotset = {
            "w1": (w1B_hbm, NCC, "wfA"),
            "w2": (w2B_hbm, NCC, "wfB"),
            "h1": (h1B_hbm, ND, "wfA"),
            "h2": (h2B_hbm, ND, "wfA"),
            "w3": (w3B_hbm, ND, "wfA"),
        }[kind]
        slot = oc % 2
        dst = (wfA if slotset == "wfA" else wfB)[slot]
        width = nin * 128
        guard = gbank[(slotset, slot)]
        ln = d_t[tw_lane[0] % len(d_t)]
        tw_lane[0] += 1
        em("sync", lambda e, d=dst[:, :width], s=hbm[oc]: e.dma_start(
            out=d, in_=s, transpose=True),
           wait=(guard, wready[kind]), inc=ln, inc_n=16)
        return dst, (ln, ln.v)

    def ffn_sweep(si, t0, SW):
        NTC = SW // 128
        TH = ceil_div(SW, 512)
        WTH = min(512, SW)
        xbuf_guard = [gbank[("xF", 0)], gbank[("xF", 1)],
                      gbank[("x2F", 0)], gbank[("x2F", 1)]]
        bufs = [o_xF, o_xF + C * 4, o_x2F, o_x2F + C * 4]
        ln_tiles = emit_ln(x1_hbm, t0, NTC, bufs, xbuf_guard, o_xln2, SW)
        ln_done = ln_tiles[-1]

        def stage(kind_a, kind_b, src_off, dst_off, nin, sem_scale_a, sem_scale_b):
            pass

        units = {}
        units["w1", 0] = get_unit("w1", 0, si)
        units["w2", 0] = get_unit("w2", 0, si)
        for oc in range(ND):
            if oc + 1 < ND:
                units["w1", oc + 1] = get_unit("w1", oc + 1, si)
                units["w2", oc + 1] = get_unit("w2", oc + 1, si)
            ua, ra = units.pop(("w1", oc))
            ub, rb = units.pop(("w2", oc))
            sevs = []
            for part, (u, r) in (("a", (ua, ra)), ("b", (ub, rb))):
                for th in range(TH):
                    bk = (0 if part == "a" else 2) + th
                    for cc in range(NCC):
                        lhs = u[:, cc * 128:(cc + 1) * 128]
                        rhs = A[:, (o_xln2 // 2) + cc * SW + th * WTH:
                                (o_xln2 // 2) + cc * SW + th * WTH + WTH]
                        em("tensor", lambda t, o=bank(bk, WTH), l=lhs, r=rhs,
                           st_=(cc == 0), sp=(cc == NCC - 1): t.matmul(
                               o, l, r, start=st_, stop=sp),
                           wait=(r if cc == 0 else None,
                                 ln_tiles[min(th * 4 + 3, NTC - 1)] if cc == 0 else None,
                                 gbank[bk] if cc == 0 else None),
                           inc=t_mm if cc == NCC - 1 else None)
                    sevs.append(((t_mm, t_mm.v), bk, th))
            gbank[("wfA", oc % 2)] = (t_mm, t_mm.v)
            gbank[("wfB", oc % 2)] = (t_mm, t_mm.v)
            for th in range(TH):
                tma = sevs[th][0]
                bka = sevs[th][1]
                tmb = sevs[TH + th][0]
                bkb = sevs[TH + th][1]
                sl = A[:, (o_silu // 2) + th * WTH: (o_silu // 2) + th * WTH + WTH]
                em("scalar", lambda sc, o=sl, i_=bank(bka, WTH), s1=w1sv[:, oc:oc + 1]:
                   sc.activation(out=o, in_=i_, func=AF.Silu, scale=s1),
                   wait=(tma, gbank[("silu", th)]), inc=s_e)
                sev = (s_e, s_e.v)
                gbank[bka] = sev
                hdst = A[:, (o_h // 2) + oc * SW + th * WTH:
                         (o_h // 2) + oc * SW + th * WTH + WTH]
                em("vector", lambda v, o=hdst, i_=bank(bkb, WTH),
                   s1=w2sv[:, oc:oc + 1], s2=sl: v.scalar_tensor_tensor(
                       out=o, in0=i_, scalar=s1, in1=s2, op0=ALU.mult, op1=ALU.mult),
                   wait=(tmb, sev), inc=v_e)
                gbank[bkb] = (v_e, v_e.v)
                gbank[("silu", th)] = (v_e, v_e.v)
        h_done = (v_e, v_e.v)

        def gstage(kind, src_o, dst_o, scalev, actdone_prev):
            units2 = {0: get_unit(kind, 0, si)}
            out_ev = None
            for oc in range(ND):
                if oc + 1 < ND:
                    units2[oc + 1] = get_unit(kind, oc + 1, si)
                u, r = units2.pop(oc)
                tms = []
                for th in range(TH):
                    bk = (oc % 2) * 2 + th
                    for dc in range(ND):
                        lhs = u[:, dc * 128:(dc + 1) * 128]
                        rhs = A[:, (src_o // 2) + dc * SW + th * WTH:
                                (src_o // 2) + dc * SW + th * WTH + WTH]
                        em("tensor", lambda t, o=bank(bk, WTH), l=lhs, r=rhs,
                           st_=(dc == 0), sp=(dc == ND - 1): t.matmul(
                               o, l, r, start=st_, stop=sp),
                           wait=(r if dc == 0 else None,
                                 actdone_prev if dc == 0 else None,
                                 gbank[bk] if dc == 0 else None),
                           inc=t_mm if dc == ND - 1 else None)
                    tms.append(((t_mm, t_mm.v), bk))
                gbank[("wfA", oc % 2)] = (t_mm, t_mm.v)
                for th in range(TH):
                    tm, bk = tms[th]
                    dst = A[:, (dst_o // 2) + oc * SW + th * WTH:
                            (dst_o // 2) + oc * SW + th * WTH + WTH]
                    em("scalar", lambda sc, o=dst, i_=bank(bk, WTH),
                       s1=scalev[:, oc:oc + 1]: sc.activation(
                           out=o, in_=i_, func=AF.Gelu, scale=s1),
                       wait=(tm,), inc=s_e)
                    gbank[bk] = (s_e, s_e.v)
                    out_ev = (s_e, s_e.v)
            return out_ev

        g1_done = gstage("h1", o_h, o_g, h1sv, h_done)
        g2_done = gstage("h2", o_g, o_h, h2sv, g1_done)

        yreg = o_xln2
        units3 = {0: get_unit("w3", 0, si)}
        for half in range(TH):
            for oc in range(NCC):
                if (half, oc) != (TH - 1, NCC - 1):
                    nxt = oc + 1 if oc + 1 < NCC else 0
                    if nxt != 0 or half + 1 < TH:
                        if nxt not in units3:
                            units3[nxt] = get_unit("w3", nxt, si)
                u, r = units3.pop(oc) if oc in units3 else get_unit("w3", oc, si)
                bk = (oc % 2) * 2
                for dc in range(ND):
                    lhs = u[:, dc * 128:(dc + 1) * 128]
                    rhs = A[:, (o_h // 2) + dc * SW + half * WTH:
                            (o_h // 2) + dc * SW + half * WTH + WTH]
                    em("tensor", lambda t, o=bank(bk, WTH), l=lhs, r=rhs,
                       st_=(dc == 0), sp=(dc == ND - 1): t.matmul(
                           o, l, r, start=st_, stop=sp),
                       wait=(r if dc == 0 else None,
                             g2_done if dc == 0 else None,
                             gbank[bk] if dc == 0 else None),
                       inc=t_mm if dc == ND - 1 else None)
                tm = (t_mm, t_mm.v)
                gbank[("wfA", oc % 2)] = (t_mm, t_mm.v)
                ydst = sb(yreg + oc * WTH * 4, WTH, F32)
                em("scalar", lambda sc, o=ydst, i_=bank(bk, WTH),
                   s1=w3sv[:, oc:oc + 1]: sc.activation(
                       out=o, in_=i_, func=AF.Identity, scale=s1),
                   wait=(tm, gbank[("y", oc)]), inc=s_e)
                gbank[bk] = (s_e, s_e.v)
                gbank[("y", oc)] = (s_e, s_e.v)
            yev = (s_e, s_e.v)
            for tci in range(WTH // 128):
                tc = half * (WTH // 128) + tci
                bb = 4 + (tc % 2) * 2
                xb = sb(o_xF + (tc % 2) * C * 4, C, F32)
                em("scalar", lambda s, t_=t0 + tc * 128, b_=xb: s.dma_start(
                    out=b_, in_=x1_hbm[t_: t_ + 128, :]),
                   wait=(xbuf_guard[tc % 2],), inc=d_x, inc_n=16)
                xld = (d_x, d_x.v)
                for oc in range(NCC):
                    inb = sb(yreg + oc * WTH * 4 + tci * 128 * 4, 128, F32)
                    outb = PS[:, (bb + oc // 4) * 512 + (oc % 4) * 128:
                              (bb + oc // 4) * 512 + (oc % 4) * 128 + 128]
                    em("tensor", lambda t, o=outb, i_=inb: t.transpose(o, i_, identf),
                       wait=(yev if oc == 0 else None, c_bbc if oc == 0 else None,
                             gbank[("yT", tc % 2)] if oc == 0 else None),
                       inc=t_tr if oc == NCC - 1 else None)
                ttr = (t_tr, t_tr.v)
                x2b = sb(o_x2F + (tc % 2) * C * 4, C, F32)
                em("vector", lambda v, o=x2b, i_=PS[:, bb * 512: bb * 512 + C], x_=xb:
                   v.tensor_tensor(out=o, in0=i_, in1=x_, op=ALU.add),
                   wait=(ttr, xld, gbank[("x2F", tc % 2)]), inc=v_res)
                ev = (v_res, v_res.v)
                gbank[("yT", tc % 2)] = ev
                xbuf_guard[tc % 2] = ev
                em("scalar", lambda s, t_=t0 + tc * 128, b_=x2b: s.dma_start(
                    out=out_hbm[t_: t_ + 128, :], in_=b_),
                   wait=(ev,), inc=d_st, inc_n=16)
                gbank[("x2F", tc % 2)] = (d_st, d_st.v)
            for oc in range(NCC):
                gbank[("y", oc)] = (v_res, v_res.v)
        gbank[("xF", 0)] = xbuf_guard[0]
        gbank[("xF", 1)] = xbuf_guard[1]

    gbank.update({("silu", 0): None, ("silu", 1): None,
                  ("xF", 0): (v_res, 0), ("xF", 1): None,
                  ("x2F", 0): None, ("x2F", 1): None,
                  ("yT", 0): x1_stores_done, ("yT", 1): x1_stores_done})
    for oc in range(NCC):
        gbank[("y", oc)] = None
    gbank[("xF", 0)] = x1_stores_done
    gbank[("xF", 1)] = x1_stores_done
    gbank[("x2F", 0)] = x1_stores_done
    gbank[("x2F", 1)] = x1_stores_done

    t0 = 0
    for si, SW in enumerate(sweeps):
        ffn_sweep(si, t0, SW)
        t0 += SW

    final_st = d_st.v
    em("sync", lambda s: s.wait_ge(d_st.h, final_st))

    with nc.Block() as block:
        def run(eng_name):
            def _(eng):
                for waits, fn, inc, inc_n in prog.q[eng_name]:
                    for s, v in waits:
                        eng.wait_ge(s.h, v)
                    inst = fn(eng)
                    if inc is not None:
                        inst.then_inc(inc.h, inc_n)
            return _
        block.tensor(run("tensor"))
        block.vector(run("vector"))
        block.scalar(run("scalar"))
        block.sync(run("sync"))
        block.gpsimd(run("gpsimd"))

    return nc, st, prog


def make_inputs(cfg, full_inputs):
    B, T, C, H = cfg["B"], cfg["T"], cfg["C"], cfg["H"]
    n_cores = 16 // B if False else None
    x = full_inputs["L0_attn_x"]
    ncores = x.shape[0] // B
    ident = np.eye(128, dtype=np.float32)
    binT = np.triu(np.ones((128, 128), dtype=np.float32), k=1) * -100000.0
    bproj_bc = np.ascontiguousarray(
        np.broadcast_to(np.asarray(full_inputs["L0_attn_b_proj"], np.float32), (128, C)))
    common = dict(
        identf=ident,
        wqkv=np.ascontiguousarray(full_inputs["L0_attn_w_qkv"], np.float32),
        wproj=np.ascontiguousarray(full_inputs["L0_attn_w_proj"], np.float32),
        bproj_bc=bproj_bc,
        w1=np.ascontiguousarray(full_inputs["L0_ffn_w1_w"], np.float32),
        w2=np.ascontiguousarray(full_inputs["L0_ffn_w2_w"], np.float32),
        h1=np.ascontiguousarray(full_inputs["L0_ffn_h1_w"], np.float32),
        h2=np.ascontiguousarray(full_inputs["L0_ffn_h2_w"], np.float32),
        w3=np.ascontiguousarray(full_inputs["L0_ffn_w3_w"], np.float32),
        w1s=np.ascontiguousarray(
            np.asarray(full_inputs["L0_ffn_w1_s"], np.float32).reshape(-1, 128).T),
        w2s=np.ascontiguousarray(
            np.asarray(full_inputs["L0_ffn_w2_s"], np.float32).reshape(-1, 128).T),
        h1s=np.ascontiguousarray(
            np.asarray(full_inputs["L0_ffn_h1_s"], np.float32).reshape(-1, 128).T),
        h2s=np.ascontiguousarray(
            np.asarray(full_inputs["L0_ffn_h2_s"], np.float32).reshape(-1, 128).T),
        w3s=np.ascontiguousarray(
            np.asarray(full_inputs["L0_ffn_w3_s"], np.float32).reshape(-1, 128).T),
        ident=ident, binT=binT,
    )
    in_maps = []
    for c in range(ncores):
        m = dict(common)
        m["x"] = np.ascontiguousarray(
            np.asarray(x[c * B:(c + 1) * B], np.float32).reshape(B * T, C))
        in_maps.append(m)
    return in_maps


def kernel(L0_attn_x, L0_attn_w_qkv, L0_attn_w_proj, L0_attn_b_proj,
           L0_ffn_w1_w, L0_ffn_w1_s, L0_ffn_w2_w, L0_ffn_w2_s,
           L0_ffn_h1_w, L0_ffn_h1_s, L0_ffn_h2_w, L0_ffn_h2_s,
           L0_ffn_w3_w, L0_ffn_w3_s):
    cfg = real_cfg()
    full = dict(L0_attn_x=L0_attn_x, L0_attn_w_qkv=L0_attn_w_qkv,
                L0_attn_w_proj=L0_attn_w_proj, L0_attn_b_proj=L0_attn_b_proj,
                L0_ffn_w1_w=L0_ffn_w1_w, L0_ffn_w1_s=L0_ffn_w1_s,
                L0_ffn_w2_w=L0_ffn_w2_w, L0_ffn_w2_s=L0_ffn_w2_s,
                L0_ffn_h1_w=L0_ffn_h1_w, L0_ffn_h1_s=L0_ffn_h1_s,
                L0_ffn_h2_w=L0_ffn_h2_w, L0_ffn_h2_s=L0_ffn_h2_s,
                L0_ffn_w3_w=L0_ffn_w3_w, L0_ffn_w3_s=L0_ffn_w3_s)
    in_maps = make_inputs(cfg, full)
    nc, stack, _prog = build(cfg)
    try:
        res = run_bass_kernel_spmd(nc, in_maps, core_ids=list(range(8)))
    finally:
        stack.close()
    B, T, C = cfg["B"], cfg["T"], cfg["C"]
    outs = [r["out"].reshape(B, T, C) for r in res.results]
    return np.concatenate(outs, axis=0)
